# version 25
# baseline (speedup 1.0000x reference)
import os
os.environ.setdefault("TRN_ALLOW_ICI_ALL_TO_ALL", "1")
import numpy as np
import concourse.bass as bass
import concourse.mybir as mybir
from concourse.bass_utils import run_bass_kernel_spmd

F32 = mybir.dt.float32
BF16 = mybir.dt.bfloat16
AF = mybir.ActivationFunctionType
ALU = mybir.AluOpType
AX = mybir.AxisListType

NEG_VIS = -1.0e30
SENT = -3.0e38
SENT_CMP = -2.0e38


class Syncer:
    def __init__(self, nc):
        self.nc = nc
        self._ctxs = []
        self.sems = {}
        self.counts = {}
        self.waited = {}
        self.floor = {}

    def sem(self, name):
        if name not in self.sems:
            cm = self.nc.semaphore(name)
            s = cm.__enter__()
            self._ctxs.append(cm)
            self.sems[name] = s
            self.counts[name] = 0
        return self.sems[name]

    def close(self):
        for cm in reversed(self._ctxs):
            cm.__exit__(None, None, None)

    DMA_PREFIX = ("ld", "dt", "st", "xt", "gt", "cc", "lp", "lda")

    def set_floor(self):
        self.floor = {k: v for k, v in self.counts.items() if v > 0}

    def wait(self, eng, eng_name, toks):
        need = dict(self.floor)
        for t in toks:
            if t is None:
                continue
            sem_name, val = t
            if sem_name.startswith(self.DMA_PREFIX):
                val = self.counts[sem_name]
            if val > need.get(sem_name, -1):
                need[sem_name] = val
        w = self.waited.setdefault(eng_name, {})
        for sem_name, val in sorted(need.items()):
            if w.get(sem_name, 0) >= val:
                continue
            eng.wait_ge(self.sems[sem_name], val)
            w[sem_name] = val

    def done(self, inst, sem_name, n=1):
        self.sem(sem_name)
        self.counts[sem_name] += n
        v = self.counts[sem_name]
        inst.then_inc(self.sems[sem_name], n)
        return (sem_name, v)


def build_fused(S, NCORES=8):
    from contextlib import ExitStack
    T = S // NCORES
    E = 2048
    HC = E // 128
    NTT = T // 128
    TBW = min(512, T)
    NTB = T // TBW
    AR = 1024
    SC_Q = float(1.0 / np.sqrt(128.0))
    nq = S // 128
    nsb = nq // 2
    WMAX = S + 128
    NTMAX = WMAX // 128
    RG = [list(range(NCORES))]

    nc = bass.Bass("TRN2", target_bir_lowering=False)
    sy = Syncer(nc)
    for s_ in ("v", "t", "sc", "g", "ld0", "ld1", "ldw0", "ldw1", "ldg0", "ldg1", "ldb", "xt", "gt",
               "stq", "stp", "sta", "st0", "st1", "cc", "ldh", "ldhk", "ldhq",
               "ldhv", "ldw2", "ldw3", "ldA0", "ldA1", "lds0", "lds1", "dt0", "dt1", "dtv",
               "lp0", "lp1", "lda"):
        sy.sem(s_)

    ins = {}
    ins["x"] = nc.dram_tensor("x", [T, E], F32, kind="ExternalInput").ap()
    for nmw in ("Wq", "Wk", "Wv", "WT", "Wo"):
        ins[nmw] = nc.dram_tensor(nmw, [E, E], F32, kind="ExternalInput").ap()
    ins["A2T"] = nc.dram_tensor("A2T", [E, AR], F32, kind="ExternalInput").ap()
    ins["B2"] = nc.dram_tensor("B2", [AR, E], F32, kind="ExternalInput").ap()
    for nmb in ("bq", "bk", "bv", "fb", "bo"):
        ins[nmb] = nc.dram_tensor(nmb, [1, E], F32, kind="ExternalInput").ap()
    ins["pk"] = nc.dram_tensor("pk", [4, 256], F32, kind="ExternalInput").ap()
    ins["pv"] = nc.dram_tensor("pv", [4, 256], F32, kind="ExternalInput").ap()
    out_ap = nc.dram_tensor("out", [T, E], F32, kind="ExternalOutput").ap()

    qkvi = nc.dram_tensor("qkvi", [NCORES * 768, T], BF16, kind="Internal").ap()
    qkvo = nc.dram_tensor("qkvo", [NCORES * 768, T], BF16, kind="Internal").ap()
    ati = nc.dram_tensor("ati", [NCORES * 256, T], BF16, kind="Internal").ap()
    wobf = nc.dram_tensor("wobf", [E, E], BF16, kind="Internal").ap()
    ato = nc.dram_tensor("ato", [NCORES * 256, T], BF16, kind="Internal").ap()
    part = nc.dram_tensor("part", [T, E], F32, kind="Internal").ap()

    with ExitStack() as octx:
        block = octx.enter_context(nc.Block())
        T_ = {}

        with ExitStack() as ctx:
            ec = ctx.enter_context
            xstg = [ec(nc.sbuf_tensor(f"xstg{i}", [128, E], F32))
                    for i in range(2)]
            xn = ec(nc.sbuf_tensor("xn", [128, E], BF16))
            xnT = ec(nc.sbuf_tensor("xnT", [128, HC * T], BF16,
                                    is_transpose_dest=True))
            Wbf = ec(nc.sbuf_tensor("Wbf", [128, HC * E], BF16))

            bstg = ec(nc.sbuf_tensor("bstg", [1, E], F32))
            bias_b = {nmb: ec(nc.sbuf_tensor(f"bias_{nmb}", [1, E], BF16))
                      for nmb in ("bq", "bk", "bv", "fb")}
            ones = ec(nc.sbuf_tensor("ones", [1, 512], BF16))
            ostg = [ec(nc.sbuf_tensor("ostg0", [128, T], BF16))]
            vecs = ec(nc.sbuf_tensor("vecs", [128, NTT * AR], BF16,
                                     is_transpose_dest=True))
            arr = ec(nc.sbuf_tensor("arr", [128, 64], F32))
            m8 = ec(nc.sbuf_tensor("m8", [128, 8], F32))
            routes = ec(nc.sbuf_tensor("routes", [128, 64], BF16))
            tmp = ec(nc.sbuf_tensor("tmpb", [128, AR], BF16))
            vsel = ec(nc.sbuf_tensor("vsel", [128, 16], F32))
            G = ec(nc.sbuf_tensor("G", [128, AR], BF16))
            delta = ec(nc.sbuf_tensor("delta", [128, HC * AR], BF16))
            wsg = [ec(nc.sbuf_tensor(f"wsg{i}", [128, AR], F32))
                   for i in range(2)]
            s1 = ec(nc.sbuf_tensor("s1", [128, 1], F32))
            s2 = ec(nc.sbuf_tensor("s2", [128, 1], F32))
            mu = ec(nc.sbuf_tensor("mu", [128, 1], F32))
            t2 = ec(nc.sbuf_tensor("t2", [128, 1], F32))
            varb = ec(nc.sbuf_tensor("varb", [128, 1], F32))
            rstd = ec(nc.sbuf_tensor("rstd", [128, 1], F32))
            nmur = ec(nc.sbuf_tensor("nmur", [128, 1], F32))
            NPS = 4
            psl = [ec(nc.psum_tensor(f"psl{i}", [128, 512], F32))
                   for i in range(8)]

            st_ = {"psum_free": [None] * 8, "xstg_free": [[], []],
                   "ostg_free": [[]],
                   "xn_free": [], "sqs_free": [], "stats_free": [],
                   "a2t_readers": [], "wsg_free": [[], []],
                   "tmp_free": [], "G_free": [], "outf_free": [],
                   "bstg_free": [], "Wbf_readers": []}

            @block.vector
            def _(v):
                T_["ones"] = sy.done(v.memset(ones[:], 1.0), "v")

            for tt in range(NTT):
                p = tt % 2

                @block.sync
                def _(s, tt=tt, p=p):
                    sy.wait(s, "s", st_["xstg_free"][p])
                    T_["xld"] = sy.done(
                        s.dma_start(out=xstg[p][:],
                                    in_=ins["x"][tt * 128:(tt + 1) * 128, :]),
                        f"ld{p}", 16)

                @block.vector
                def _(v, tt=tt, p=p):
                    sy.wait(v, "v", [T_["xld"]] + st_["stats_free"])
                    T_["s1"] = sy.done(v.reduce_sum(out=s1[:], in_=xstg[p][:],
                                                    axis=AX.X), "v")

                @block.scalar
                def _(sc, tt=tt, p=p):
                    sy.wait(sc, "sc", [T_["xld"]] + st_["xn_free"]
                            + st_["stats_free"])
                    T_["s2"] = sy.done(sc.activation(out=xn[:], in_=xstg[p][:],
                                                     func=AF.Square, bias=0.0,
                                                     scale=1.0,
                                                     accum_out=s2[:]), "sc")

                @block.vector
                def _(v, tt=tt, p=p):
                    sy.wait(v, "v", [T_["s1"], T_["s2"]])
                    a1 = sy.done(v.tensor_scalar_mul(mu[:], s1[:], 1.0 / E), "v")
                    sy.wait(v, "v", [a1])
                    a2 = sy.done(v.tensor_mul(t2[:], mu[:], mu[:]), "v")
                    a3 = sy.done(v.tensor_scalar_mul(varb[:], s2[:], 1.0 / E), "v")
                    sy.wait(v, "v", [a2, a3])
                    a4 = sy.done(v.tensor_sub(varb[:], varb[:], t2[:]), "v")
                    sy.wait(v, "v", [a4])
                    a5 = sy.done(v.tensor_scalar_add(varb[:], varb[:], 1e-5), "v")
                    sy.wait(v, "v", [a5])
                    T_["rcp"] = sy.done(v.reciprocal(out=rstd[:], in_=varb[:]), "v")

                @block.scalar
                def _(sc, tt=tt, p=p):
                    sy.wait(sc, "sc", [T_["rcp"]])
                    b1 = sy.done(sc.activation(out=rstd[:], in_=rstd[:],
                                               func=AF.Sqrt, bias=0.0,
                                               scale=1.0), "sc")
                    sy.wait(sc, "sc", [b1])
                    T_["nmr"] = sy.done(sc.activation(out=nmur[:], in_=mu[:],
                                                      func=AF.Identity, bias=0.0,
                                                      scale=rstd[:]), "sc")

                @block.vector
                def _(v, tt=tt, p=p):
                    sy.wait(v, "v", [T_["nmr"]])
                    T_["nmn"] = sy.done(v.tensor_scalar_mul(nmur[:], nmur[:],
                                                            -1.0), "v")

                @block.scalar
                def _(sc, tt=tt, p=p):
                    sy.wait(sc, "sc", [T_["nmn"], T_["s2"]] + st_["xn_free"])
                    T_["xn"] = sy.done(sc.activation(out=xn[:], in_=xstg[p][:],
                                                     func=AF.Identity,
                                                     bias=nmur[:],
                                                     scale=rstd[:]), "sc")
                    st_["xstg_free"][p] = [T_["xn"], T_["s1"], T_["s2"]]
                    st_["stats_free"] = [T_["xn"]]

                @block.sync
                def _(s, tt=tt):
                    sy.wait(s, "s", [T_["xn"]])
                    xv = xnT[:].rearrange("c (hc t) -> c hc t", t=T)
                    tr = s.dma_start_transpose(
                        out=xv[:, :, tt * 128:(tt + 1) * 128],
                        in_=xn[:].rearrange("q (hc c) -> q hc c", c=128))
                    T_["xnT"] = sy.done(tr, "xt", 16)
                    st_["xn_free"] = [T_["xnT"]]

            xnT_tok = T_["xnT"]

            def load_weights(src_ap, nchunk, width, dst=None, rkey="Wbf_readers",
                             sem="ldg"):
                if dst is None:
                    dst = Wbf
                prev_readers = st_[rkey]
                st_[rkey] = []
                toks = []
                for i in range(nchunk):
                    p = i % 2

                    @block.gpsimd
                    def _(g, i=i, p=p):
                        sy.wait(g, "g", prev_readers)
                        toks.append(sy.done(
                            g.dma_start(
                                out=dst[:, i * width:(i + 1) * width],
                                in_=src_ap[i * 128:(i + 1) * 128, :]),
                            f"{sem}{p}", 16))
                return toks

            def staged_chunk(src_rows, dst_cols, k, prev_readers):
                p = k % 2

                @block.scalar
                def _(sc, p=p):
                    sy.wait(sc, "sc", st_["wsg_free"][p])
                    T_["wsld"] = sy.done(
                        sc.dma_start(out=wsg[p][:], in_=src_rows),
                        f"lds{p}", 16)

                out_tok = []

                @block.vector
                def _(v, p=p):
                    sy.wait(v, "v", [T_["wsld"]] + prev_readers)
                    ck = sy.done(v.tensor_copy(out=dst_cols, in_=wsg[p][:]),
                                 "v")
                    st_["wsg_free"][p] = [ck]
                    out_tok.append(ck)
                return out_tok[0]

            def load_bias(name):
                @block.sync
                def _(s, name=name):
                    sy.wait(s, "s", st_["bstg_free"])
                    T_["bld"] = sy.done(s.dma_start(out=bstg[:], in_=ins[name]),
                                        "ldb", 16)

                @block.vector
                def _(v, name=name):
                    sy.wait(v, "v", [T_["bld"]])
                    tok = sy.done(v.tensor_copy(out=bias_b[name][:],
                                                in_=bstg[:]), "v")
                    st_["bstg_free"] = [tok]
                    T_[f"b_{name}"] = tok

            qkv_stores = []
            for pi, (wname, bname, off, scale) in enumerate(
                    [("Wq", "bq", 0, SC_Q), ("Wk", "bk", 256, 1.0),
                     ("Wv", "bv", 512, 1.0)]):
                wt = load_weights(ins[wname], HC, E)
                load_bias(bname)
                for lt in range(HC):
                    po = 0
                    ev_toks = []
                    for tb in range(NTB):
                        bank = (lt * NTB + tb) % NPS

                        @block.tensor
                        def _(te, lt=lt, tb=tb, bank=bank, bname=bname, wt=wt):
                            sy.wait(te, "t", wt + [T_[f"b_{bname}"], T_["ones"],
                                                   xnT_tok,
                                                   st_["psum_free"][bank]])
                            for hc in range(HC):
                                te.matmul(
                                    out=psl[bank][:, 0:TBW],
                                    lhsT=Wbf[:, hc * E + lt * 128:
                                             hc * E + (lt + 1) * 128],
                                    rhs=xnT[:, hc * T + tb * TBW:
                                            hc * T + (tb + 1) * TBW],
                                    start=(hc == 0), stop=False)
                            mm = te.matmul(
                                out=psl[bank][:, 0:TBW],
                                lhsT=bias_b[bname][:, lt * 128:(lt + 1) * 128],
                                rhs=ones[:, 0:TBW], start=False, stop=True)
                            T_["pmm"] = sy.done(mm, "t")
                            st_["Wbf_readers"].append(T_["pmm"])

                        @block.scalar
                        def _(sc, lt=lt, tb=tb, bank=bank, po=po, scale=scale):
                            sy.wait(sc, "sc", [T_["pmm"]] + st_["ostg_free"][po])
                            ev = sc.activation(
                                out=ostg[po][:, tb * TBW:(tb + 1) * TBW],
                                in_=psl[bank][:, 0:TBW], func=AF.Copy,
                                bias=0.0, scale=scale)
                            tok = sy.done(ev, "sc")
                            st_["psum_free"][bank] = tok
                            ev_toks.append(tok)

                    @block.sync
                    def _(s, lt=lt, po=po, off=off, ev_toks=ev_toks):
                        sy.wait(s, "s", ev_toks)
                        r, hh = lt // 2, lt % 2
                        base = r * 768 + off + hh * 128
                        stt = s.dma_start(out=qkvi[base:base + 128, :],
                                          in_=ostg[po][:, 0:T])
                        tok = sy.done(stt, "stq", 16)
                        st_["ostg_free"][po] = [tok]
                        qkv_stores.append(tok)

            @block.gpsimd
            def _(g):
                sy.wait(g, "g", qkv_stores)
                cc = g.collective_compute("AllToAll", ALU.bypass,
                                          replica_groups=RG,
                                          ins=[qkvi.opt()], outs=[qkvo.opt()])
                T_["cc1"] = sy.done(cc, "cc", 1)

            wtA = []
            for i in range(HC):
                wtA.append(staged_chunk(
                    ins["A2T"][i * 128:(i + 1) * 128, :],
                    delta[:, i * AR:(i + 1) * AR], i, []))
            st_["a2t_readers"] = []
            vec_toks = []
            for tt in range(NTT):
                for tb in range(AR // 512):
                    bank = (tt * 2 + tb) % NPS

                    @block.tensor
                    def _(te, tt=tt, tb=tb, bank=bank, wtA=wtA):
                        sy.wait(te, "t", wtA + [xnT_tok, st_["psum_free"][bank]])
                        for hc in range(HC):
                            mm = te.matmul(
                                out=psl[bank][:],
                                lhsT=xnT[:, hc * T + tt * 128:
                                         hc * T + (tt + 1) * 128],
                                rhs=delta[:, hc * AR + tb * 512:
                                          hc * AR + (tb + 1) * 512],
                                start=(hc == 0), stop=(hc == HC - 1))
                        T_["vmm"] = sy.done(mm, "t")
                        st_["a2t_readers"].append(T_["vmm"])

                    @block.scalar
                    def _(sc, tt=tt, tb=tb, bank=bank):
                        sy.wait(sc, "sc", [T_["vmm"]])
                        ev = sc.activation(out=vecs[:, tt * AR + tb * 512:
                                                    tt * AR + (tb + 1) * 512],
                                           in_=psl[bank][:], func=AF.Copy,
                                           bias=0.0, scale=1.0)
                        tok = sy.done(ev, "sc")
                        st_["psum_free"][bank] = tok
                        vec_toks.append(tok)

            wt = load_weights(ins["WT"], HC, E)
            load_bias("fb")
            wtB = []

            def emit_b2(k):
                rc, h = k // 2, k % 2
                wtB.append(staged_chunk(
                    ins["B2"][rc * 128:(rc + 1) * 128,
                              h * AR:(h + 1) * AR],
                    delta[:, rc * E + h * AR:rc * E + (h + 1) * AR], k,
                    st_["a2t_readers"]))

            for tt in range(NTT):
                vslice = vecs[:, tt * AR:(tt + 1) * AR]

                @block.scalar
                def _(sc, tt=tt, vslice=vslice):
                    sy.wait(sc, "sc", vec_toks[2 * tt:2 * tt + 2]
                            + st_["xn_free"] + st_["sqs_free"])
                    T_["sq"] = sy.done(sc.activation(out=xn[:, 0:AR], in_=vslice,
                                                     func=AF.Square, bias=0.0,
                                                     scale=1.0), "sc")

                @block.vector
                def _(v, tt=tt, vslice=vslice):
                    sy.wait(v, "v", [T_["sq"]])
                    c1 = sy.done(v.reduce_sum(
                        out=arr[:],
                        in_=xn[:, 0:AR].rearrange("p (a r) -> p a r", r=16),
                        axis=AX.X), "v")
                    sy.wait(v, "v", [c1])
                    c2 = sy.done(v.max(out=m8[:], in_=arr[:]), "v")
                    sy.wait(v, "v", [c2])
                    c3 = sy.done(v.tensor_scalar(out=routes[:], in0=arr[:],
                                                 scalar1=m8[:, 3:4], scalar2=0.25,
                                                 op0=ALU.is_ge,
                                                 op1=ALU.mult), "v")
                    sy.wait(v, "v", [c3] + st_["tmp_free"])
                    rb = routes[:].rearrange("p (a o) -> p a o", o=1) \
                        .to_broadcast([128, 64, 16])
                    c4 = sy.done(v.tensor_tensor(
                        out=tmp[:].rearrange("p (a r) -> p a r", r=16),
                        in0=vslice.rearrange("p (a r) -> p a r", r=16),
                        in1=rb, op=ALU.mult), "v")
                    sy.wait(v, "v", [c4])
                    c5 = sy.done(v.reduce_sum(
                        out=vsel[:],
                        in_=tmp[:].rearrange("p (a r) -> p r a", r=16),
                        axis=AX.X), "v")
                    sy.wait(v, "v", [c5] + st_["G_free"])
                    vb2 = vsel[:].rearrange("p (o r) -> p o r", o=1) \
                        .to_broadcast([128, 64, 16])
                    rb2 = routes[:].rearrange("p (a o) -> p a o", o=1) \
                        .to_broadcast([128, 64, 16])
                    T_["G"] = sy.done(v.tensor_tensor(
                        out=G[:].rearrange("p (a r) -> p a r", r=16),
                        in0=rb2, in1=vb2, op=ALU.mult), "v")
                    st_["sqs_free"] = [c1]
                    st_["xn_free"] = [c1]
                    st_["tmp_free"] = [c5]

                for k in range(tt * 16 // NTT, (tt + 1) * 16 // NTT):
                    emit_b2(k)

                @block.sync
                def _(s, tt=tt):
                    sy.wait(s, "s", [T_["G"]])
                    tr = s.dma_start_transpose(
                        out=vecs[:, tt * AR:(tt + 1) * AR]
                        .rearrange("c (rc q) -> c rc q", q=128),
                        in_=G[:].rearrange("q (rc c) -> q rc c", c=128))
                    T_[f"GTt{tt}"] = sy.done(tr, "gt", 16)
                    st_["G_free"] = [T_[f"GTt{tt}"]]

            for tt in range(NTT):
                p = tt % 2

                @block.sync
                def _(s, tt=tt, p=p):
                    sy.wait(s, "s", st_["xstg_free"][p])
                    T_["xld2"] = sy.done(
                        s.dma_start(out=xstg[p][:],
                                    in_=ins["x"][tt * 128:(tt + 1) * 128, :]),
                        f"ld{p}", 16)

                @block.tensor
                def _(te, tt=tt, wt=wt, wtB=wtB):
                    sy.wait(te, "t", wt + [T_["b_fb"], T_["ones"]]
                            + [st_["psum_free"][b] for b in range(4)])
                    for jb in range(E // 512):
                        for hc in range(HC):
                            te.matmul(
                                out=psl[jb][:],
                                lhsT=xnT[:, hc * T + tt * 128:
                                         hc * T + (tt + 1) * 128],
                                rhs=Wbf[:, hc * E + jb * 512:
                                        hc * E + (jb + 1) * 512],
                                start=(hc == 0), stop=False)
                        mm = te.matmul(out=psl[jb][:], lhsT=ones[:, 0:128],
                                       rhs=bias_b["fb"][:, jb * 512:
                                                        (jb + 1) * 512],
                                       start=False, stop=True)
                    T_["bmm"] = sy.done(mm, "t")
                    st_["Wbf_readers"].append(T_["bmm"])
                    sy.wait(te, "t", wtB + [T_[f"GTt{tt}"]]
                            + [st_["psum_free"][4 + b] for b in range(4)])
                    for jb in range(E // 512):
                        for rc in range(AR // 128):
                            mm = te.matmul(
                                out=psl[4 + jb][:],
                                lhsT=vecs[:, tt * AR + rc * 128:
                                          tt * AR + (rc + 1) * 128],
                                rhs=delta[:, rc * E + jb * 512:
                                          rc * E + (jb + 1) * 512],
                                start=(rc == 0), stop=(rc == AR // 128 - 1))
                    T_["dmm"] = sy.done(mm, "t")

                cmb_toks = []
                for jb in range(E // 512):

                    @block.vector
                    def _(v, tt=tt, jb=jb, p=p):
                        sy.wait(v, "v", [T_["bmm"], T_["xld2"]])
                        sl = xstg[p][:, jb * 512:(jb + 1) * 512]
                        o1 = sy.done(v.tensor_add(sl, sl, psl[jb][:]), "v")
                        sy.wait(v, "v", [o1, T_["dmm"]])
                        o2 = sy.done(v.tensor_add(sl, sl, psl[4 + jb][:]), "v")
                        st_["psum_free"][jb] = o2
                        st_["psum_free"][4 + jb] = o2
                        cmb_toks.append(o2)

                @block.sync
                def _(s, tt=tt, p=p, cmb_toks=cmb_toks):
                    sy.wait(s, "s", cmb_toks)
                    stt = s.dma_start(out=part[tt * 128:(tt + 1) * 128, :],
                                      in_=xstg[p][:])
                    tok = sy.done(stt, "stp", 16)
                    st_["xstg_free"][p] = [tok] + cmb_toks
                    T_["pstore"] = tok

        sy.set_floor()
        with ExitStack() as ctx:
            ec = ctx.enter_context
            qT = ec(nc.sbuf_tensor("qT", [128, S], BF16))
            kT = ec(nc.sbuf_tensor("kTp", [128, WMAX], BF16))
            vt = ec(nc.sbuf_tensor("vt", [128, NTMAX * 128], BF16,
                                   is_transpose_dest=True))
            sA2 = [ec(nc.sbuf_tensor(f"sA{i}", [128, WMAX], BF16))
                   for i in range(2)]
            sB2 = [ec(nc.sbuf_tensor(f"sB{i}", [128, WMAX], BF16))
                   for i in range(2)]
            e2 = [ec(nc.sbuf_tensor(f"e{i}", [128, WMAX], BF16))
                  for i in range(2)]
            pT = ec(nc.sbuf_tensor("pT", [128, NTMAX * 256], BF16,
                                   is_transpose_dest=True))
            r8 = ec(nc.sbuf_tensor("r8", [128, 64], F32))
            nm = ec(nc.sbuf_tensor("nm", [128, 1], F32))
            zz = ec(nc.sbuf_tensor("zz", [128, 2], F32))
            rz = ec(nc.sbuf_tensor("rz", [128, 2], F32))
            rrow = ec(nc.sbuf_tensor("rrow", [1, 256], F32))
            ones1 = ec(nc.sbuf_tensor("ones1", [1, 128], F32))
            Rt = ec(nc.sbuf_tensor("Rt", [128, 256], F32))
            iidx = ec(nc.sbuf_tensor("iidx", [128, 1536], F32))
            mask = ec(nc.sbuf_tensor("maskt", [128, 1536], BF16))
            identf = ec(nc.sbuf_tensor("identf", [128, 128], F32))
            identb = ec(nc.sbuf_tensor("identb", [128, 128], BF16))
            pstg = ec(nc.sbuf_tensor("pstg", [128, 128], F32))
            pstgb = ec(nc.sbuf_tensor("pstgb", [128, 128], BF16))
            att = ec(nc.sbuf_tensor("att", [128, 256], BF16))
            NPS = 4
            ps_s = [ec(nc.psum_tensor(f"ps_s{i}", [128, 512], F32))
                    for i in range(NPS)]
            ps_o = ec(nc.psum_tensor("ps_o", [128, 256], F32))
            ps_r = ec(nc.psum_tensor("ps_r", [1, 128], F32))
            ps_R = ec(nc.psum_tensor("ps_R", [128, 256], F32))
            ps_t = ec(nc.psum_tensor("ps_t", [128, 128], BF16))

            T = S // NCORES

            @block.vector
            def _(v):
                sy.wait(v, "v", [])

            @block.scalar
            def _(sc):
                sy.wait(sc, "sc", [])

            @block.tensor
            def _(te):
                sy.wait(te, "t", [])

            @block.gpsimd
            def _(g):
                sy.wait(g, "g", [])

            @block.sync
            def _(s):
                sy.wait(s, "s", [])

            @block.gpsimd
            def _(g):
                T_["iota"] = sy.done(
                    g.iota(iidx[:], pattern=[[-1, 1536]], base=4,
                           channel_multiplier=1,
                           allow_small_or_imprecise_dtypes=True), "g")

            @block.vector
            def _(v):
                sy.wait(v, "v", [T_["iota"]])
                T_["mask"] = sy.done(v.tensor_scalar(out=mask[:], in0=iidx[:],
                                                     scalar1=0.0,
                                                     scalar2=NEG_VIS,
                                                     op0=ALU.is_lt,
                                                     op1=ALU.mult), "v")
                T_["identf"] = sy.done(v.tensor_scalar(out=identf[:],
                                                       in0=iidx[:, 0:128],
                                                       scalar1=4.0, scalar2=None,
                                                       op0=ALU.is_equal), "v")
                T_["ones1"] = sy.done(v.memset(ones1[:], 1.0), "v")
                sy.wait(v, "v", [T_["identf"]])
                T_["identb"] = sy.done(v.tensor_copy(out=identb[:],
                                                     in_=identf[:]), "v")

            st2 = {"sA_free": [[], []], "sB_free": [[], []],
                   "e_free": [[], []],
                   "pT_free": [],
                   "psum_free": [None] * NPS, "zz_free": [], "rz_free": [],
                   "rrow_free": [], "psR_free": [], "pso_free": [],
                   "att_free": [], "psr_free": [], "nm_free": [],
                   "r8_free": [], "vstage_free": []}

            att_stores = []
            for hh in range(2):
                head_gate = [] if hh else [T_["cc1"]]

                @block.gpsimd
                def _(g, hh=hh, head_gate=head_gate):
                    sy.wait(g, "g", head_gate + st2.get("qkT_readers", []))
                    for j in range(NCORES):
                        b = j * 768 + hh * 128
                        tk = sy.done(g.dma_start(out=kT[:, j * T:(j + 1) * T],
                                                 in_=qkvo[b + 256:b + 384, :]),
                                     "ldhk", 16)
                        tq = sy.done(g.dma_start(out=qT[:, j * T:(j + 1) * T],
                                                 in_=qkvo[b:b + 128, :]),
                                     "ldhq", 16)
                    sy.wait(g, "g", st2["vstage_free"] + st2["sA_free"][0])
                    for j in range(NCORES):
                        b = j * 768 + hh * 128
                        tok = sy.done(g.dma_start(
                            out=sA2[0][:, j * T:(j + 1) * T],
                            in_=qkvo[b + 512:b + 640, :]), "ldhv", 16)
                    T_["hload_kq"] = [tk, tq]
                    T_["hload"] = tok

                @block.sync
                def _(s, hh=hh):
                    sy.wait(s, "s", [T_["hload"]] + st2["pT_free"])
                    tr = s.dma_start_transpose(
                        out=vt[:].rearrange("c (ct d) -> c ct d",
                                            d=128)[:, 0:S // 128, :],
                        in_=sA2[0][:, 0:S].rearrange("d (ct c) -> d ct c",
                                                     c=128))
                    T_["vtt"] = sy.done(tr, "dtv", 16)
                    st2["vstage_free"] = [T_["vtt"]]
                    st2["sA_free"][0] = st2["sA_free"][0] + [T_["vtt"]]

                @block.vector
                def _(v, hh=hh, head_gate=head_gate):
                    sy.wait(v, "v", head_gate + [T_.get("pk_stg_free")])
                    T_["pkz"] = sy.done(v.memset(pstg[:], 0.0), "v")

                @block.sync
                def _(s, hh=hh):
                    sy.wait(s, "s", [T_["pkz"]])
                    T_["pkld"] = sy.done(
                        s.dma_start(out=pstg[0:4, :],
                                    in_=ins["pk"][:, hh * 128:(hh + 1) * 128]),
                        "ldb", 16)

                @block.vector
                def _(v, hh=hh):
                    sy.wait(v, "v", [T_["pkld"], T_.get("pkt_prev")])
                    T_["pkb"] = sy.done(v.tensor_copy(out=pstgb[:],
                                                      in_=pstg[:]), "v")

                @block.tensor
                def _(te, hh=hh):
                    sy.wait(te, "t", [T_["pkb"], T_["identb"],
                                      T_.get("pkt_evac_prev")])
                    T_["pkt"] = sy.done(te.transpose(out=ps_t[:], in_=pstgb[:],
                                                     identity=identb[:]), "t")

                @block.vector
                def _(v, hh=hh):
                    sy.wait(v, "v", [T_["pkt"]] + T_["hload_kq"][:1])
                    T_["pk_in_kT"] = sy.done(v.tensor_copy(out=kT[:, S:S + 4],
                                                           in_=ps_t[:, 0:4]),
                                             "v")
                    T_["kT_zpad"] = sy.done(v.memset(kT[:, S + 4:S + 128], 0.0),
                                            "v")
                    sy.wait(v, "v", [T_["vtt"]])
                    T_["vt_z"] = sy.done(v.memset(vt[:, S:S + 128], 0.0), "v")
                    sy.wait(v, "v", [T_["pkt"]])
                    T_["pvz"] = sy.done(v.memset(pstg[:], 0.0), "v")
                T_["pkt_evac_prev"] = T_["pk_in_kT"]

                @block.sync
                def _(s, hh=hh):
                    sy.wait(s, "s", [T_["pvz"]])
                    T_["pvld"] = sy.done(
                        s.dma_start(out=pstg[0:4, :],
                                    in_=ins["pv"][:, hh * 128:(hh + 1) * 128]),
                        "ldb", 16)

                @block.vector
                def _(v, hh=hh):
                    sy.wait(v, "v", [T_["pvld"], T_["vt_z"]])
                    T_["pv_in_vt"] = sy.done(
                        v.tensor_copy(out=vt[0:4, S:S + 128],
                                      in_=pstg[0:4, :]), "v")
                T_["pk_stg_free"] = T_["pv_in_vt"]
                T_["pkt_prev"] = T_["pkb"]

                head_ready = T_["hload_kq"] + [T_["pk_in_kT"],
                                               T_["kT_zpad"]]
                pv_ready = [T_["vtt"], T_["vt_z"], T_["pv_in_vt"]]

                pend_epilogue = [None]

                def flush_epilogue():
                    if pend_epilogue[0] is not None:
                        fn = pend_epilogue[0]
                        pend_epilogue[0] = None
                        fn()

                tiles = []
                for sb in range(nsb):
                    qts = [2 * sb, 2 * sb + 1]
                    Wtrue = 128 * (qts[1] + 2)
                    W = min((Wtrue + 511) // 512 * 512, S + 128)
                    for j, qt in enumerate(qts):
                        tiles.append((sb, j, qt, W))

                def emit_scores_evac(idx):
                    sb, j, qt, W = tiles[idx]
                    pa = (sb * 2 + j) % 2
                    sA = sA2[pa]
                    nblk = (W + 511) // 512
                    mm_toks = []
                    ev_toks = []

                    @block.tensor
                    def _(te, qt=qt, nblk=nblk, W=W, mm_toks=mm_toks):
                        sy.wait(te, "t", head_ready)
                        sc_base = sy.counts["sc"]
                        for b in range(nblk):
                            c0, c1 = b * 512, min((b + 1) * 512, W)
                            if b < NPS:
                                sy.wait(te, "t", [st2["psum_free"][b % NPS]])
                            else:
                                sy.wait(te, "t", [("sc", sc_base + b - NPS + 1)])
                            mm = te.matmul(
                                out=ps_s[b % NPS][:, 0:c1 - c0],
                                lhsT=qT[:, qt * 128:(qt + 1) * 128],
                                rhs=kT[:, c0:c1],
                                start=True, stop=True)
                            mm_toks.append(sy.done(mm, "t"))
                            st2["qkT_readers"] = [mm_toks[-1]]

                    @block.scalar
                    def _(sc, nblk=nblk, W=W, sA=sA, pa=pa,
                          mm_toks=mm_toks, ev_toks=ev_toks):
                        sy.wait(sc, "sc", st2["sA_free"][pa])
                        for b in range(nblk):
                            c0, c1 = b * 512, min((b + 1) * 512, W)
                            sy.wait(sc, "sc", [mm_toks[b]])
                            ev = sc.activation(out=sA[:, c0:c1],
                                               in_=ps_s[b % NPS][:, 0:c1 - c0],
                                               func=AF.Copy, bias=0.0,
                                               scale=1.0)
                            tok = sy.done(ev, "sc")
                            st2["psum_free"][b % NPS] = tok
                            ev_toks.append(tok)
                    return mm_toks, ev_toks

                pend_se = emit_scores_evac(0)
                gate_toks = []
                tr_toks = []
                for idx in range(len(tiles)):
                    sb, j, qt, W = tiles[idx]
                    pa = (sb * 2 + j) % 2
                    sA = sA2[pa]
                    sB = sB2[pa]
                    e = e2[pa]
                    Wx = min(128 * (qt + 2), S + 128)
                    ntc = W // 128
                    if j == 0:
                        gate_toks = []
                        tr_toks = []
                    mm_toks, ev_toks = pend_se

                    @block.vector
                    def _(v, j=j, qt=qt, W=W, Wx=Wx, ev_toks=ev_toks,
                          sA=sA, pa=pa, e=e):
                        mw = Wx - 128 * qt
                        sy.wait(v, "v", ev_toks + [T_["mask"]])
                        tmask = sy.done(v.tensor_add(sA[:, Wx - mw:Wx],
                                                     sA[:, Wx - mw:Wx],
                                                     mask[:, 0:mw]), "v")
                        st2["sA_free"][pa] = []
                        if Wx < W:
                            sy.wait(v, "v", st2["e_free"][pa])
                            tz = sy.done(v.memset(e[:, Wx:W], 0.0), "v")
                            st2["e_free"][pa] = [tz]
                        sy.wait(v, "v", [tmask] + st2["r8_free"])
                        tmx = sy.done(v.max(out=r8[:, 0:8],
                                            in_=sA[:, 0:Wx]), "v")
                        sy.wait(v, "v", [tmx] + st2["nm_free"])
                        tnm = sy.done(v.tensor_scalar_mul(nm[:], r8[:, 0:1],
                                                          -1.0), "v")
                        sy.wait(v, "v", st2["sB_free"][pa])
                        tmr = sy.done(v.match_replace(
                            out=sB[:, 0:Wx], in_to_replace=r8[:, 0:8],
                            in_values=sA[:, 0:Wx], imm_value=SENT), "v")
                        for rnd in range(1, 8):
                            sy.wait(v, "v", [tmr])
                            tmx = sy.done(v.max(out=r8[:, 8 * rnd:8 * rnd + 8],
                                                in_=sB[:, 0:Wx]), "v")
                            sy.wait(v, "v", [tmx])
                            tmr = sy.done(v.match_replace(
                                out=sB[:, 0:Wx],
                                in_to_replace=r8[:, 8 * rnd:8 * rnd + 8],
                                in_values=sB[:, 0:Wx], imm_value=SENT), "v")
                        st2["sB_free"][pa] = []
                        st2["r8_free"] = [tmr]
                        T_["tnm"], T_["tmr8"] = tnm, tmr

                    if idx + 1 < len(tiles):
                        pend_se = emit_scores_evac(idx + 1)

                    @block.scalar
                    def _(sc, W=W, Wx=Wx, sA=sA, e=e, pa=pa):
                        sy.wait(sc, "sc", [T_["tnm"]] + st2["e_free"][pa])
                        ex = sc.activation(out=e[:, 0:Wx], in_=sA[:, 0:Wx],
                                           func=AF.Exp, bias=nm[:],
                                           scale=1.0)
                        T_["exp"] = sy.done(ex, "sc")
                        st2["e_free"][pa] = []
                        st2["nm_free"] = [T_["exp"]]
                        st2["sA_free"][pa] = [T_["exp"], T_["tmr8"]]

                    @block.vector
                    def _(v, j=j, W=W, Wx=Wx, e=e, sB=sB, pa=pa):
                        sy.wait(v, "v", [T_["exp"], T_["tmr8"]]
                                + st2["zz_free"])
                        g_ = v.scalar_tensor_tensor(
                            out=e[:, 0:Wx], in0=sB[:, 0:Wx],
                            scalar=SENT_CMP,
                            in1=e[:, 0:Wx], op0=ALU.is_lt, op1=ALU.mult,
                            accum_out=zz[:, j:j + 1])
                        T_["gate"] = sy.done(g_, "v")
                        gate_toks.append(T_["gate"])
                        st2["sB_free"][pa] = [T_["gate"]]

                    if j == 0:
                        flush_epilogue()

                    @block.sync
                    def _(s, j=j, W=W, ntc=ntc, e=e, pa=pa):
                        sy.wait(s, "s", [T_["gate"]] + st2["pT_free"])
                        tr = s.dma_start_transpose(
                            out=pT[:].rearrange("c (t j q) -> c t j q",
                                                j=2, q=128)[:, 0:ntc, j, :],
                            in_=e[:, 0:W].rearrange("q (t c) -> q t c",
                                                    c=128))
                        T_["tr"] = sy.done(tr, f"dt{pa}", 16)
                        tr_toks.append(T_["tr"])
                        st2["e_free"][pa] = [T_["tr"]]

                    if j != 1:
                        continue

                    @block.vector
                    def _(v, gate_toks=gate_toks):
                        sy.wait(v, "v", gate_toks + st2["rz_free"])
                        T_["rz"] = sy.done(v.reciprocal(out=rz[:], in_=zz[:]),
                                           "v")
                        st2["zz_free"] = [T_["rz"]]

                    def make_epilogue(hh=hh, sb=sb, ntc=ntc,
                                      tr_toks=tr_toks, rz_tok=None):
                        rz_tok = T_["rz"]

                        def emit():
                            for j in range(2):
                                @block.tensor
                                def _(te, j=j):
                                    sy.wait(te, "t", [rz_tok, T_["identf"]]
                                            + st2["psr_free"])
                                    T_[f"psr{j}"] = sy.done(
                                        te.transpose(out=ps_r[:],
                                                     in_=rz[:, j:j + 1],
                                                     identity=identf[:]), "t")

                                @block.scalar
                                def _(sc, j=j):
                                    sy.wait(sc, "sc", [T_[f"psr{j}"]]
                                            + st2["rrow_free"])
                                    ev = sc.activation(
                                        out=rrow[0:1, j * 128:(j + 1) * 128],
                                        in_=ps_r[:], func=AF.Copy, bias=0.0,
                                        scale=1.0)
                                    T_[f"rrow{j}"] = sy.done(ev, "sc")
                                    st2["psr_free"] = [T_[f"rrow{j}"]]

                            st2["rz_free"] = [T_["psr0"], T_["psr1"]]
                            st2["rrow_free"] = []

                            @block.tensor
                            def _(te):
                                sy.wait(te, "t", [T_["rrow0"], T_["rrow1"],
                                                  T_["ones1"]]
                                        + st2["psR_free"])
                                for j in range(2):
                                    mm = te.matmul(
                                        out=ps_R[:, j * 128:(j + 1) * 128],
                                        lhsT=ones1[:],
                                        rhs=rrow[0:1, j * 128:(j + 1) * 128],
                                        start=True, stop=True)
                                T_["Rmm"] = sy.done(mm, "t")
                                st2["rrow_free"] = [T_["Rmm"]]
                                sy.wait(te, "t", tr_toks + st2["pso_free"]
                                        + pv_ready)
                                for ct in range(ntc):
                                    mm = te.matmul(
                                        out=ps_o[:],
                                        lhsT=vt[:, ct * 128:(ct + 1) * 128],
                                        rhs=pT[:, ct * 256:(ct + 1) * 256],
                                        start=(ct == 0),
                                        stop=(ct == ntc - 1))
                                T_["pv"] = sy.done(mm, "t")
                                st2["pT_free"] = [T_["pv"]]
                            Rmm = T_["Rmm"]
                            pv = T_["pv"]

                            @block.vector
                            def _(v):
                                sy.wait(v, "v", [Rmm] + st2.get("Rt_free", []))
                                tR = sy.done(v.tensor_copy(out=Rt[:],
                                                           in_=ps_R[:]), "v")
                                st2["psR_free"] = [tR]
                                sy.wait(v, "v", [tR, pv] + st2["att_free"])
                                ta = sy.done(v.tensor_mul(att[:], ps_o[:],
                                                          Rt[:]), "v")
                                T_["att"] = ta
                                st2["pso_free"] = [ta]
                                st2["Rt_free"] = [ta]

                            @block.sync
                            def _(s):
                                sy.wait(s, "s", [T_["att"]])
                                spr = T // 256
                                r, c0 = sb // spr, (sb % spr) * 256
                                stt = s.dma_start(
                                    out=ati[r * 256 + hh * 128:
                                            r * 256 + (hh + 1) * 128,
                                            c0:c0 + 256],
                                    in_=att[:])
                                st2_t = sy.done(stt, "sta", 16)
                                T_["store"] = st2_t
                                att_stores.append(st2_t)
                                st2["att_free"] = [st2_t]
                        return emit

                    pend_epilogue[0] = make_epilogue()

                flush_epilogue()
                T_["headdone"] = T_["store"]

            @block.gpsimd
            def _(g):
                wo_toks = []
                for i in range(HC):
                    wo_toks.append(sy.done(
                        g.dma_start(out=wobf[i * 128:(i + 1) * 128, :],
                                    in_=ins["Wo"][i * 128:(i + 1) * 128, :]),
                        f"ldg{i % 2}", 16))
                T_["wocast"] = wo_toks[-2:]

            @block.gpsimd
            def _(g):
                sy.wait(g, "g", att_stores)
                cc = g.collective_compute("AllToAll", ALU.bypass,
                                          replica_groups=RG,
                                          ins=[ati.opt()], outs=[ato.opt()])
                T_["cc2"] = sy.done(cc, "cc", 1)

        sy.set_floor()
        with ExitStack() as ctx:
            ec = ctx.enter_context
            aT = ec(nc.sbuf_tensor("aTs", [128, HC * T], BF16))
            Wbf3 = ec(nc.sbuf_tensor("Wbf3", [128, HC * E], BF16))

            bstg3 = ec(nc.sbuf_tensor("bstg3", [1, E], F32))
            bob = ec(nc.sbuf_tensor("bob", [1, E], BF16))
            ones3 = ec(nc.sbuf_tensor("ones3", [1, 128], BF16))
            pstg3 = [ec(nc.sbuf_tensor(f"pstg3{i}", [128, E], F32))
                     for i in range(2)]
            outf3 = [ec(nc.sbuf_tensor(f"outf3{i}", [128, E], F32))
                     for i in range(2)]
            NPS = 4
            psl3 = [ec(nc.psum_tensor(f"psl3{i}", [128, 512], F32))
                    for i in range(NPS)]

            st3 = {"psum_free": [None] * NPS,
                   "pstg_free": [[], []], "outf_free": [[], []]}

            @block.vector
            def _(v):
                sy.wait(v, "v", [])

            @block.scalar
            def _(sc):
                sy.wait(sc, "sc", [])

            @block.tensor
            def _(te):
                sy.wait(te, "t", [])

            @block.gpsimd
            def _(g):
                sy.wait(g, "g", [])

            @block.sync
            def _(s):
                sy.wait(s, "s", [])

            @block.vector
            def _(v):
                T_["ones3"] = sy.done(v.memset(ones3[:], 1.0), "v")

            @block.gpsimd
            def _(g):
                sy.wait(g, "g", [T_["cc2"]])
                for i in range(4):
                    tok = sy.done(g.dma_start(
                        out=aT[:].rearrange("c (hc t) -> c hc t",
                                            t=T)[:, i * 4:(i + 1) * 4, :],
                        in_=ato.rearrange("(hc c) t -> c hc t",
                                          c=128)[:, i * 4:(i + 1) * 4, :]),
                        "lda", 16)
                T_["ald"] = tok

            @block.sync
            def _(s):
                sy.wait(s, "s", [T_["pstore"]])
                T_["bld3"] = sy.done(s.dma_start(out=bstg3[:], in_=ins["bo"]),
                                     "ldb", 16)

            @block.vector
            def _(v):
                sy.wait(v, "v", [T_["bld3"]])
                T_["bo"] = sy.done(v.tensor_copy(out=bob[:], in_=bstg3[:]), "v")

            wls = []
            for i in range(HC):
                if i % 2 == 0:
                    @block.sync
                    def _(s, i=i):
                        if i == 0:
                            sy.wait(s, "s", T_["wocast"])
                        wls.append(sy.done(
                            s.dma_start(out=Wbf3[:, i * E:(i + 1) * E],
                                        in_=wobf[i * 128:(i + 1) * 128, :]),
                            "ldw0", 16))
                else:
                    @block.scalar
                    def _(sc, i=i):
                        if i == 1:
                            sy.wait(sc, "sc", T_["wocast"])
                        wls.append(sy.done(
                            sc.dma_start(out=Wbf3[:, i * E:(i + 1) * E],
                                         in_=wobf[i * 128:(i + 1) * 128, :]),
                            "ldw1", 16))

            for tt in range(T // 128):
                p = tt % 2

                @block.scalar
                def _(sc, tt=tt, p=p):
                    sy.wait(sc, "sc", st3["pstg_free"][p])
                    T_["pld"] = sy.done(
                        sc.dma_start(out=pstg3[p][:],
                                     in_=part[tt * 128:(tt + 1) * 128, :]),
                        f"lp{p}", 16)

                cmb = []
                for jb in range(E // 512):
                    bank = (tt * 4 + jb) % NPS

                    @block.tensor
                    def _(te, tt=tt, jb=jb, bank=bank):
                        sy.wait(te, "t", wls + [T_["ald"], T_["bo"], T_["ones3"],
                                                st3["psum_free"][bank]])
                        for hc in range(HC):
                            te.matmul(
                                out=psl3[bank][:],
                                lhsT=aT[:, hc * T + tt * 128:
                                        hc * T + (tt + 1) * 128],
                                rhs=Wbf3[:, hc * E + jb * 512:
                                         hc * E + (jb + 1) * 512],
                                start=(hc == 0), stop=False)
                        mm = te.matmul(out=psl3[bank][:], lhsT=ones3[:],
                                       rhs=bob[:, jb * 512:(jb + 1) * 512],
                                       start=False, stop=True)
                        T_["omm"] = sy.done(mm, "t")

                    @block.vector
                    def _(v, tt=tt, jb=jb, bank=bank, p=p):
                        sy.wait(v, "v", [T_["omm"], T_["pld"]]
                                + st3["outf_free"][p])
                        o1 = sy.done(v.tensor_add(
                            outf3[p][:, jb * 512:(jb + 1) * 512], psl3[bank][:],
                            pstg3[p][:, jb * 512:(jb + 1) * 512]), "v")
                        st3["psum_free"][bank] = o1
                        cmb.append(o1)

                @block.sync
                def _(s, tt=tt, p=p, cmb=cmb):
                    sy.wait(s, "s", cmb)
                    stt = s.dma_start(out=out_ap[tt * 128:(tt + 1) * 128, :],
                                      in_=outf3[p][:])
                    tok = sy.done(stt, f"st{p}", 16)
                    st3["outf_free"][p] = [tok]
                    st3["pstg_free"][p] = [tok] + cmb
                    T_[f"ostore{p}"] = tok

            @block.sync
            def _(s):
                sy.wait(s, "s", [T_["ostore0"], T_["ostore1"]])

    sy.close()
    return nc


S_FULL = 8192
E_FULL = 2048
NCORES = 8


def kernel(L0_attn_x, L0_attn_Wq, L0_attn_bq, L0_attn_Wk, L0_attn_bk,
           L0_attn_Wv, L0_attn_bv, L0_attn_Wo, L0_attn_bo, L0_attn_pk,
           L0_attn_pv, L0_ffn_A, L0_ffn_B, L0_ffn_W, L0_ffn_bias):
    S, E = S_FULL, E_FULL
    Tl = S // NCORES
    x = np.ascontiguousarray(np.asarray(L0_attn_x, dtype=np.float32)
                             .reshape(S, E))
    A2T = np.ascontiguousarray(
        np.asarray(L0_ffn_A, dtype=np.float32).reshape(1024, E).T)
    B2 = np.ascontiguousarray(
        np.asarray(L0_ffn_B, dtype=np.float32).transpose(0, 2, 1).reshape(1024, E))
    WT = np.ascontiguousarray(np.asarray(L0_ffn_W, dtype=np.float32).T)
    Wq = np.ascontiguousarray(np.asarray(L0_attn_Wq, dtype=np.float32))
    Wk = np.ascontiguousarray(np.asarray(L0_attn_Wk, dtype=np.float32))
    Wv = np.ascontiguousarray(np.asarray(L0_attn_Wv, dtype=np.float32))
    Wo = np.ascontiguousarray(np.asarray(L0_attn_Wo, dtype=np.float32))
    bs = {"bq": L0_attn_bq, "bk": L0_attn_bk, "bv": L0_attn_bv,
          "fb": L0_ffn_bias, "bo": L0_attn_bo}
    bs = {k: np.ascontiguousarray(np.asarray(v, dtype=np.float32)
                                  .reshape(1, E)) for k, v in bs.items()}
    pk = np.ascontiguousarray(np.asarray(L0_attn_pk, dtype=np.float32)
                              .reshape(4, E))
    pv = np.ascontiguousarray(np.asarray(L0_attn_pv, dtype=np.float32)
                              .reshape(4, E))

    nc = build_fused(S, NCORES)
    maps = []
    for c in range(NCORES):
        m = {"x": np.ascontiguousarray(x[c * Tl:(c + 1) * Tl]),
             "Wq": Wq, "Wk": Wk, "Wv": Wv, "WT": WT, "Wo": Wo,
             "A2T": A2T, "B2": B2,
             "pk": np.ascontiguousarray(pk[:, c * 256:(c + 1) * 256]),
             "pv": np.ascontiguousarray(pv[:, c * 256:(c + 1) * 256])}
        m.update(bs)
        maps.append(m)
    res = run_bass_kernel_spmd(nc, maps, core_ids=list(range(NCORES)))
    out = np.concatenate([np.asarray(res.results[c]["out"])
                          for c in range(NCORES)], axis=0)
    return out.reshape(1, S, E).astype(np.float32)


# version 26
# speedup vs baseline: 1.0048x; 1.0048x over previous
import os
os.environ.setdefault("TRN_ALLOW_ICI_ALL_TO_ALL", "1")
import numpy as np
import concourse.bass as bass
import concourse.mybir as mybir
from concourse.bass_utils import run_bass_kernel_spmd

F32 = mybir.dt.float32
BF16 = mybir.dt.bfloat16
AF = mybir.ActivationFunctionType
ALU = mybir.AluOpType
AX = mybir.AxisListType

NEG_VIS = -1.0e30
SENT = -3.0e38
SENT_CMP = -2.0e38


class Syncer:
    def __init__(self, nc):
        self.nc = nc
        self._ctxs = []
        self.sems = {}
        self.counts = {}
        self.waited = {}
        self.floor = {}

    def sem(self, name):
        if name not in self.sems:
            cm = self.nc.semaphore(name)
            s = cm.__enter__()
            self._ctxs.append(cm)
            self.sems[name] = s
            self.counts[name] = 0
        return self.sems[name]

    def close(self):
        for cm in reversed(self._ctxs):
            cm.__exit__(None, None, None)

    DMA_PREFIX = ("ld", "dt", "st", "xt", "gt", "cc", "lp", "lda")

    def set_floor(self):
        self.floor = {k: v for k, v in self.counts.items() if v > 0}

    def wait(self, eng, eng_name, toks):
        need = dict(self.floor)
        for t in toks:
            if t is None:
                continue
            sem_name, val = t
            if sem_name.startswith(self.DMA_PREFIX):
                val = self.counts[sem_name]
            if val > need.get(sem_name, -1):
                need[sem_name] = val
        w = self.waited.setdefault(eng_name, {})
        for sem_name, val in sorted(need.items()):
            if w.get(sem_name, 0) >= val:
                continue
            eng.wait_ge(self.sems[sem_name], val)
            w[sem_name] = val

    def done(self, inst, sem_name, n=1):
        self.sem(sem_name)
        self.counts[sem_name] += n
        v = self.counts[sem_name]
        inst.then_inc(self.sems[sem_name], n)
        return (sem_name, v)


def build_fused(S, NCORES=8):
    from contextlib import ExitStack
    T = S // NCORES
    E = 2048
    HC = E // 128
    NTT = T // 128
    TBW = min(512, T)
    NTB = T // TBW
    AR = 1024
    SC_Q = float(1.0 / np.sqrt(128.0))
    nq = S // 128
    nsb = nq // 2
    WMAX = S + 128
    NTMAX = WMAX // 128
    RG = [list(range(NCORES))]

    nc = bass.Bass("TRN2", target_bir_lowering=False)
    sy = Syncer(nc)
    for s_ in ("v", "t", "sc", "g", "ld0", "ld1", "ldw0", "ldw1", "ldg0", "ldg1", "ldb", "xt", "gt",
               "stq", "stp", "sta", "st0", "st1", "cc", "ldh", "ldhk", "ldhq",
               "ldhv", "ldw2", "ldw3", "ldA0", "ldA1", "lds0", "lds1", "dt0", "dt1", "dtv",
               "lp0", "lp1", "lda"):
        sy.sem(s_)

    ins = {}
    ins["x"] = nc.dram_tensor("x", [T, E], F32, kind="ExternalInput").ap()
    for nmw in ("Wq", "Wk", "Wv", "WT", "Wo"):
        ins[nmw] = nc.dram_tensor(nmw, [E, E], F32, kind="ExternalInput").ap()
    ins["A2T"] = nc.dram_tensor("A2T", [E, AR], F32, kind="ExternalInput").ap()
    ins["B2"] = nc.dram_tensor("B2", [AR, E], F32, kind="ExternalInput").ap()
    for nmb in ("bq", "bk", "bv", "fb", "bo"):
        ins[nmb] = nc.dram_tensor(nmb, [1, E], F32, kind="ExternalInput").ap()
    ins["pk"] = nc.dram_tensor("pk", [4, 256], F32, kind="ExternalInput").ap()
    ins["pv"] = nc.dram_tensor("pv", [4, 256], F32, kind="ExternalInput").ap()
    out_ap = nc.dram_tensor("out", [T, E], F32, kind="ExternalOutput").ap()

    qkvi = nc.dram_tensor("qkvi", [NCORES * 768, T], BF16, kind="Internal").ap()
    qkvo = nc.dram_tensor("qkvo", [NCORES * 768, T], BF16, kind="Internal").ap()
    ati = nc.dram_tensor("ati", [NCORES * 256, T], BF16, kind="Internal").ap()
    wobf = nc.dram_tensor("wobf", [E, E], BF16, kind="Internal").ap()
    ato = nc.dram_tensor("ato", [NCORES * 256, T], BF16, kind="Internal").ap()
    part = nc.dram_tensor("part", [T, E], F32, kind="Internal").ap()

    with ExitStack() as octx:
        block = octx.enter_context(nc.Block())
        T_ = {}

        with ExitStack() as ctx:
            ec = ctx.enter_context
            xstg = [ec(nc.sbuf_tensor(f"xstg{i}", [128, E], F32))
                    for i in range(2)]
            xn = ec(nc.sbuf_tensor("xn", [128, E], BF16))
            xnT = ec(nc.sbuf_tensor("xnT", [128, HC * T], BF16,
                                    is_transpose_dest=True))
            Wbf = ec(nc.sbuf_tensor("Wbf", [128, HC * E], BF16))

            bstg = ec(nc.sbuf_tensor("bstg", [1, E], F32))
            bias_b = {nmb: ec(nc.sbuf_tensor(f"bias_{nmb}", [1, E], BF16))
                      for nmb in ("bq", "bk", "bv", "fb")}
            ones = ec(nc.sbuf_tensor("ones", [1, 512], BF16))
            ostg = [ec(nc.sbuf_tensor("ostg0", [128, T], BF16))]
            vecs = ec(nc.sbuf_tensor("vecs", [128, NTT * AR], BF16,
                                     is_transpose_dest=True))
            arr = ec(nc.sbuf_tensor("arr", [128, 64], F32))
            m8 = ec(nc.sbuf_tensor("m8", [128, 8], F32))
            routes = ec(nc.sbuf_tensor("routes", [128, 64], BF16))
            tmp = ec(nc.sbuf_tensor("tmpb", [128, AR], BF16))
            vsel = ec(nc.sbuf_tensor("vsel", [128, 16], F32))
            G = ec(nc.sbuf_tensor("G", [128, AR], BF16))
            delta = ec(nc.sbuf_tensor("delta", [128, HC * AR], BF16))
            wsg = [ec(nc.sbuf_tensor(f"wsg{i}", [128, AR], F32))
                   for i in range(2)]
            s1 = ec(nc.sbuf_tensor("s1", [128, 1], F32))
            s2 = ec(nc.sbuf_tensor("s2", [128, 1], F32))
            mu = ec(nc.sbuf_tensor("mu", [128, 1], F32))
            t2 = ec(nc.sbuf_tensor("t2", [128, 1], F32))
            varb = ec(nc.sbuf_tensor("varb", [128, 1], F32))
            rstd = ec(nc.sbuf_tensor("rstd", [128, 1], F32))
            nmur = ec(nc.sbuf_tensor("nmur", [128, 1], F32))
            NPS = 4
            psl = [ec(nc.psum_tensor(f"psl{i}", [128, 512], F32))
                   for i in range(8)]

            st_ = {"psum_free": [None] * 8, "xstg_free": [[], []],
                   "ostg_free": [[]],
                   "xn_free": [], "sqs_free": [], "stats_free": [],
                   "a2t_readers": [], "wsg_free": [[], []],
                   "tmp_free": [], "G_free": [], "outf_free": [],
                   "bstg_free": [], "Wbf_readers": []}

            @block.vector
            def _(v):
                T_["ones"] = sy.done(v.memset(ones[:], 1.0), "v")

            for tt in range(NTT):
                p = tt % 2

                @block.sync
                def _(s, tt=tt, p=p):
                    sy.wait(s, "s", st_["xstg_free"][p])
                    T_["xld"] = sy.done(
                        s.dma_start(out=xstg[p][:],
                                    in_=ins["x"][tt * 128:(tt + 1) * 128, :]),
                        f"ld{p}", 16)

                @block.vector
                def _(v, tt=tt, p=p):
                    sy.wait(v, "v", [T_["xld"]] + st_["stats_free"])
                    T_["s1"] = sy.done(v.reduce_sum(out=s1[:], in_=xstg[p][:],
                                                    axis=AX.X), "v")

                @block.scalar
                def _(sc, tt=tt, p=p):
                    sy.wait(sc, "sc", [T_["xld"]] + st_["xn_free"]
                            + st_["stats_free"])
                    T_["s2"] = sy.done(sc.activation(out=xn[:], in_=xstg[p][:],
                                                     func=AF.Square, bias=0.0,
                                                     scale=1.0,
                                                     accum_out=s2[:]), "sc")

                @block.vector
                def _(v, tt=tt, p=p):
                    sy.wait(v, "v", [T_["s1"], T_["s2"]])
                    a1 = sy.done(v.tensor_scalar_mul(mu[:], s1[:], 1.0 / E), "v")
                    sy.wait(v, "v", [a1])
                    a2 = sy.done(v.tensor_mul(t2[:], mu[:], mu[:]), "v")
                    a3 = sy.done(v.tensor_scalar_mul(varb[:], s2[:], 1.0 / E), "v")
                    sy.wait(v, "v", [a2, a3])
                    a4 = sy.done(v.tensor_sub(varb[:], varb[:], t2[:]), "v")
                    sy.wait(v, "v", [a4])
                    a5 = sy.done(v.tensor_scalar_add(varb[:], varb[:], 1e-5), "v")
                    sy.wait(v, "v", [a5])
                    T_["rcp"] = sy.done(v.reciprocal(out=rstd[:], in_=varb[:]), "v")

                @block.scalar
                def _(sc, tt=tt, p=p):
                    sy.wait(sc, "sc", [T_["rcp"]])
                    b1 = sy.done(sc.activation(out=rstd[:], in_=rstd[:],
                                               func=AF.Sqrt, bias=0.0,
                                               scale=1.0), "sc")
                    sy.wait(sc, "sc", [b1])
                    T_["nmr"] = sy.done(sc.activation(out=nmur[:], in_=mu[:],
                                                      func=AF.Identity, bias=0.0,
                                                      scale=rstd[:]), "sc")

                @block.vector
                def _(v, tt=tt, p=p):
                    sy.wait(v, "v", [T_["nmr"]])
                    T_["nmn"] = sy.done(v.tensor_scalar_mul(nmur[:], nmur[:],
                                                            -1.0), "v")

                @block.scalar
                def _(sc, tt=tt, p=p):
                    sy.wait(sc, "sc", [T_["nmn"], T_["s2"]] + st_["xn_free"])
                    T_["xn"] = sy.done(sc.activation(out=xn[:], in_=xstg[p][:],
                                                     func=AF.Identity,
                                                     bias=nmur[:],
                                                     scale=rstd[:]), "sc")
                    st_["xstg_free"][p] = [T_["xn"], T_["s1"], T_["s2"]]
                    st_["stats_free"] = [T_["xn"]]

                @block.sync
                def _(s, tt=tt):
                    sy.wait(s, "s", [T_["xn"]])
                    xv = xnT[:].rearrange("c (hc t) -> c hc t", t=T)
                    tr = s.dma_start_transpose(
                        out=xv[:, :, tt * 128:(tt + 1) * 128],
                        in_=xn[:].rearrange("q (hc c) -> q hc c", c=128))
                    T_["xnT"] = sy.done(tr, "xt", 16)
                    st_["xn_free"] = [T_["xnT"]]

            xnT_tok = T_["xnT"]

            def load_weights(src_ap, nchunk, width, dst=None, rkey="Wbf_readers",
                             sem="ldg"):
                if dst is None:
                    dst = Wbf
                prev_readers = st_[rkey]
                st_[rkey] = []
                toks = []
                for i in range(nchunk):
                    p = i % 2

                    @block.gpsimd
                    def _(g, i=i, p=p):
                        sy.wait(g, "g", prev_readers)
                        toks.append(sy.done(
                            g.dma_start(
                                out=dst[:, i * width:(i + 1) * width],
                                in_=src_ap[i * 128:(i + 1) * 128, :]),
                            f"{sem}{p}", 16))
                return toks

            def staged_chunk(src_rows, dst_cols, k, prev_readers,
                             on_sync=False):
                p = k % 2

                if on_sync:
                    @block.sync
                    def _(s, p=p):
                        sy.wait(s, "s", st_["wsg_free"][p])
                        T_["wsld"] = sy.done(
                            s.dma_start(out=wsg[p][:], in_=src_rows),
                            f"lds{p}", 16)
                else:
                    @block.scalar
                    def _(sc, p=p):
                        sy.wait(sc, "sc", st_["wsg_free"][p])
                        T_["wsld"] = sy.done(
                            sc.dma_start(out=wsg[p][:], in_=src_rows),
                            f"lds{p}", 16)

                out_tok = []

                @block.vector
                def _(v, p=p):
                    sy.wait(v, "v", [T_["wsld"]] + prev_readers)
                    ck = sy.done(v.tensor_copy(out=dst_cols, in_=wsg[p][:]),
                                 "v")
                    st_["wsg_free"][p] = [ck]
                    out_tok.append(ck)
                return out_tok[0]

            def load_bias(name):
                @block.sync
                def _(s, name=name):
                    sy.wait(s, "s", st_["bstg_free"])
                    T_["bld"] = sy.done(s.dma_start(out=bstg[:], in_=ins[name]),
                                        "ldb", 16)

                @block.vector
                def _(v, name=name):
                    sy.wait(v, "v", [T_["bld"]])
                    tok = sy.done(v.tensor_copy(out=bias_b[name][:],
                                                in_=bstg[:]), "v")
                    st_["bstg_free"] = [tok]
                    T_[f"b_{name}"] = tok

            wtA = []
            for i in range(HC):
                wtA.append(staged_chunk(
                    ins["A2T"][i * 128:(i + 1) * 128, :],
                    delta[:, i * AR:(i + 1) * AR], i, []))
            st_["a2t_readers"] = []

            qkv_stores = []
            for pi, (wname, bname, off, scale) in enumerate(
                    [("Wq", "bq", 0, SC_Q), ("Wk", "bk", 256, 1.0),
                     ("Wv", "bv", 512, 1.0)]):
                wt = load_weights(ins[wname], HC, E)
                load_bias(bname)
                for lt in range(HC):
                    po = 0
                    ev_toks = []
                    for tb in range(NTB):
                        bank = (lt * NTB + tb) % NPS

                        @block.tensor
                        def _(te, lt=lt, tb=tb, bank=bank, bname=bname, wt=wt):
                            sy.wait(te, "t", wt + [T_[f"b_{bname}"], T_["ones"],
                                                   xnT_tok,
                                                   st_["psum_free"][bank]])
                            for hc in range(HC):
                                te.matmul(
                                    out=psl[bank][:, 0:TBW],
                                    lhsT=Wbf[:, hc * E + lt * 128:
                                             hc * E + (lt + 1) * 128],
                                    rhs=xnT[:, hc * T + tb * TBW:
                                            hc * T + (tb + 1) * TBW],
                                    start=(hc == 0), stop=False)
                            mm = te.matmul(
                                out=psl[bank][:, 0:TBW],
                                lhsT=bias_b[bname][:, lt * 128:(lt + 1) * 128],
                                rhs=ones[:, 0:TBW], start=False, stop=True)
                            T_["pmm"] = sy.done(mm, "t")
                            st_["Wbf_readers"].append(T_["pmm"])

                        @block.scalar
                        def _(sc, lt=lt, tb=tb, bank=bank, po=po, scale=scale):
                            sy.wait(sc, "sc", [T_["pmm"]] + st_["ostg_free"][po])
                            ev = sc.activation(
                                out=ostg[po][:, tb * TBW:(tb + 1) * TBW],
                                in_=psl[bank][:, 0:TBW], func=AF.Copy,
                                bias=0.0, scale=scale)
                            tok = sy.done(ev, "sc")
                            st_["psum_free"][bank] = tok
                            ev_toks.append(tok)

                    @block.sync
                    def _(s, lt=lt, po=po, off=off, ev_toks=ev_toks):
                        sy.wait(s, "s", ev_toks)
                        r, hh = lt // 2, lt % 2
                        base = r * 768 + off + hh * 128
                        stt = s.dma_start(out=qkvi[base:base + 128, :],
                                          in_=ostg[po][:, 0:T])
                        tok = sy.done(stt, "stq", 16)
                        st_["ostg_free"][po] = [tok]
                        qkv_stores.append(tok)

            @block.gpsimd
            def _(g):
                sy.wait(g, "g", qkv_stores)
                cc = g.collective_compute("AllToAll", ALU.bypass,
                                          replica_groups=RG,
                                          ins=[qkvi.opt()], outs=[qkvo.opt()])
                T_["cc1"] = sy.done(cc, "cc", 1)

            vec_toks = []
            for tt in range(NTT):
                for tb in range(AR // 512):
                    bank = (tt * 2 + tb) % NPS

                    @block.tensor
                    def _(te, tt=tt, tb=tb, bank=bank, wtA=wtA):
                        sy.wait(te, "t", wtA + [xnT_tok, st_["psum_free"][bank]])
                        for hc in range(HC):
                            mm = te.matmul(
                                out=psl[bank][:],
                                lhsT=xnT[:, hc * T + tt * 128:
                                         hc * T + (tt + 1) * 128],
                                rhs=delta[:, hc * AR + tb * 512:
                                          hc * AR + (tb + 1) * 512],
                                start=(hc == 0), stop=(hc == HC - 1))
                        T_["vmm"] = sy.done(mm, "t")
                        st_["a2t_readers"].append(T_["vmm"])

                    @block.scalar
                    def _(sc, tt=tt, tb=tb, bank=bank):
                        sy.wait(sc, "sc", [T_["vmm"]])
                        ev = sc.activation(out=vecs[:, tt * AR + tb * 512:
                                                    tt * AR + (tb + 1) * 512],
                                           in_=psl[bank][:], func=AF.Copy,
                                           bias=0.0, scale=1.0)
                        tok = sy.done(ev, "sc")
                        st_["psum_free"][bank] = tok
                        vec_toks.append(tok)

            wt = load_weights(ins["WT"], HC, E)
            load_bias("fb")
            wtB = []

            def emit_b2(k):
                rc, h = k // 2, k % 2
                wtB.append(staged_chunk(
                    ins["B2"][rc * 128:(rc + 1) * 128,
                              h * AR:(h + 1) * AR],
                    delta[:, rc * E + h * AR:rc * E + (h + 1) * AR], k,
                    st_["a2t_readers"], on_sync=True))

            for tt in range(NTT):
                vslice = vecs[:, tt * AR:(tt + 1) * AR]

                @block.scalar
                def _(sc, tt=tt, vslice=vslice):
                    sy.wait(sc, "sc", vec_toks[2 * tt:2 * tt + 2]
                            + st_["xn_free"] + st_["sqs_free"])
                    T_["sq"] = sy.done(sc.activation(out=xn[:, 0:AR], in_=vslice,
                                                     func=AF.Square, bias=0.0,
                                                     scale=1.0), "sc")

                @block.vector
                def _(v, tt=tt, vslice=vslice):
                    sy.wait(v, "v", [T_["sq"]])
                    c1 = sy.done(v.reduce_sum(
                        out=arr[:],
                        in_=xn[:, 0:AR].rearrange("p (a r) -> p a r", r=16),
                        axis=AX.X), "v")
                    sy.wait(v, "v", [c1])
                    c2 = sy.done(v.max(out=m8[:], in_=arr[:]), "v")
                    sy.wait(v, "v", [c2])
                    c3 = sy.done(v.tensor_scalar(out=routes[:], in0=arr[:],
                                                 scalar1=m8[:, 3:4], scalar2=0.25,
                                                 op0=ALU.is_ge,
                                                 op1=ALU.mult), "v")
                    sy.wait(v, "v", [c3] + st_["tmp_free"])
                    rb = routes[:].rearrange("p (a o) -> p a o", o=1) \
                        .to_broadcast([128, 64, 16])
                    c4 = sy.done(v.tensor_tensor(
                        out=tmp[:].rearrange("p (a r) -> p a r", r=16),
                        in0=vslice.rearrange("p (a r) -> p a r", r=16),
                        in1=rb, op=ALU.mult), "v")
                    sy.wait(v, "v", [c4])
                    c5 = sy.done(v.reduce_sum(
                        out=vsel[:],
                        in_=tmp[:].rearrange("p (a r) -> p r a", r=16),
                        axis=AX.X), "v")
                    sy.wait(v, "v", [c5] + st_["G_free"])
                    vb2 = vsel[:].rearrange("p (o r) -> p o r", o=1) \
                        .to_broadcast([128, 64, 16])
                    rb2 = routes[:].rearrange("p (a o) -> p a o", o=1) \
                        .to_broadcast([128, 64, 16])
                    T_["G"] = sy.done(v.tensor_tensor(
                        out=G[:].rearrange("p (a r) -> p a r", r=16),
                        in0=rb2, in1=vb2, op=ALU.mult), "v")
                    st_["sqs_free"] = [c1]
                    st_["xn_free"] = [c1]
                    st_["tmp_free"] = [c5]

                for k in range(tt * 16 // NTT, (tt + 1) * 16 // NTT):
                    emit_b2(k)

                @block.sync
                def _(s, tt=tt):
                    sy.wait(s, "s", [T_["G"]])
                    tr = s.dma_start_transpose(
                        out=vecs[:, tt * AR:(tt + 1) * AR]
                        .rearrange("c (rc q) -> c rc q", q=128),
                        in_=G[:].rearrange("q (rc c) -> q rc c", c=128))
                    T_[f"GTt{tt}"] = sy.done(tr, "gt", 16)
                    st_["G_free"] = [T_[f"GTt{tt}"]]

            for tt in range(NTT):
                p = tt % 2

                @block.sync
                def _(s, tt=tt, p=p):
                    sy.wait(s, "s", st_["xstg_free"][p])
                    T_["xld2"] = sy.done(
                        s.dma_start(out=xstg[p][:],
                                    in_=ins["x"][tt * 128:(tt + 1) * 128, :]),
                        f"ld{p}", 16)

                @block.tensor
                def _(te, tt=tt, wt=wt, wtB=wtB):
                    sy.wait(te, "t", wt + [T_["b_fb"], T_["ones"]]
                            + [st_["psum_free"][b] for b in range(4)])
                    for jb in range(E // 512):
                        for hc in range(HC):
                            te.matmul(
                                out=psl[jb][:],
                                lhsT=xnT[:, hc * T + tt * 128:
                                         hc * T + (tt + 1) * 128],
                                rhs=Wbf[:, hc * E + jb * 512:
                                        hc * E + (jb + 1) * 512],
                                start=(hc == 0), stop=False)
                        mm = te.matmul(out=psl[jb][:], lhsT=ones[:, 0:128],
                                       rhs=bias_b["fb"][:, jb * 512:
                                                        (jb + 1) * 512],
                                       start=False, stop=True)
                    T_["bmm"] = sy.done(mm, "t")
                    st_["Wbf_readers"].append(T_["bmm"])
                    sy.wait(te, "t", wtB + [T_[f"GTt{tt}"]]
                            + [st_["psum_free"][4 + b] for b in range(4)])
                    for jb in range(E // 512):
                        for rc in range(AR // 128):
                            mm = te.matmul(
                                out=psl[4 + jb][:],
                                lhsT=vecs[:, tt * AR + rc * 128:
                                          tt * AR + (rc + 1) * 128],
                                rhs=delta[:, rc * E + jb * 512:
                                          rc * E + (jb + 1) * 512],
                                start=(rc == 0), stop=(rc == AR // 128 - 1))
                    T_["dmm"] = sy.done(mm, "t")

                cmb_toks = []
                for jb in range(E // 512):

                    @block.vector
                    def _(v, tt=tt, jb=jb, p=p):
                        sy.wait(v, "v", [T_["bmm"], T_["xld2"]])
                        sl = xstg[p][:, jb * 512:(jb + 1) * 512]
                        o1 = sy.done(v.tensor_add(sl, sl, psl[jb][:]), "v")
                        sy.wait(v, "v", [o1, T_["dmm"]])
                        o2 = sy.done(v.tensor_add(sl, sl, psl[4 + jb][:]), "v")
                        st_["psum_free"][jb] = o2
                        st_["psum_free"][4 + jb] = o2
                        cmb_toks.append(o2)

                @block.sync
                def _(s, tt=tt, p=p, cmb_toks=cmb_toks):
                    sy.wait(s, "s", cmb_toks)
                    stt = s.dma_start(out=part[tt * 128:(tt + 1) * 128, :],
                                      in_=xstg[p][:])
                    tok = sy.done(stt, "stp", 16)
                    st_["xstg_free"][p] = [tok] + cmb_toks
                    T_["pstore"] = tok

        sy.set_floor()
        with ExitStack() as ctx:
            ec = ctx.enter_context
            qT = ec(nc.sbuf_tensor("qT", [128, S], BF16))
            kT = ec(nc.sbuf_tensor("kTp", [128, WMAX], BF16))
            vt = ec(nc.sbuf_tensor("vt", [128, NTMAX * 128], BF16,
                                   is_transpose_dest=True))
            sA2 = [ec(nc.sbuf_tensor(f"sA{i}", [128, WMAX], BF16))
                   for i in range(2)]
            sB2 = [ec(nc.sbuf_tensor(f"sB{i}", [128, WMAX], BF16))
                   for i in range(2)]
            e2 = [ec(nc.sbuf_tensor(f"e{i}", [128, WMAX], BF16))
                  for i in range(2)]
            pT = ec(nc.sbuf_tensor("pT", [128, NTMAX * 256], BF16,
                                   is_transpose_dest=True))
            r8 = ec(nc.sbuf_tensor("r8", [128, 64], F32))
            nm = ec(nc.sbuf_tensor("nm", [128, 1], F32))
            zz = ec(nc.sbuf_tensor("zz", [128, 2], F32))
            rz = ec(nc.sbuf_tensor("rz", [128, 2], F32))
            rrow = ec(nc.sbuf_tensor("rrow", [1, 256], F32))
            ones1 = ec(nc.sbuf_tensor("ones1", [1, 128], F32))
            Rt = ec(nc.sbuf_tensor("Rt", [128, 256], F32))
            iidx = ec(nc.sbuf_tensor("iidx", [128, 1536], F32))
            mask = ec(nc.sbuf_tensor("maskt", [128, 1536], BF16))
            identf = ec(nc.sbuf_tensor("identf", [128, 128], F32))
            identb = ec(nc.sbuf_tensor("identb", [128, 128], BF16))
            pstg = ec(nc.sbuf_tensor("pstg", [128, 128], F32))
            pstgb = ec(nc.sbuf_tensor("pstgb", [128, 128], BF16))
            att = ec(nc.sbuf_tensor("att", [128, 256], BF16))
            NPS = 4
            ps_s = [ec(nc.psum_tensor(f"ps_s{i}", [128, 512], F32))
                    for i in range(NPS)]
            ps_o = ec(nc.psum_tensor("ps_o", [128, 256], F32))
            ps_r = ec(nc.psum_tensor("ps_r", [1, 128], F32))
            ps_R = ec(nc.psum_tensor("ps_R", [128, 256], F32))
            ps_t = ec(nc.psum_tensor("ps_t", [128, 128], BF16))

            T = S // NCORES

            @block.vector
            def _(v):
                sy.wait(v, "v", [])

            @block.scalar
            def _(sc):
                sy.wait(sc, "sc", [])

            @block.tensor
            def _(te):
                sy.wait(te, "t", [])

            @block.gpsimd
            def _(g):
                sy.wait(g, "g", [])

            @block.sync
            def _(s):
                sy.wait(s, "s", [])

            @block.gpsimd
            def _(g):
                T_["iota"] = sy.done(
                    g.iota(iidx[:], pattern=[[-1, 1536]], base=4,
                           channel_multiplier=1,
                           allow_small_or_imprecise_dtypes=True), "g")

            @block.vector
            def _(v):
                sy.wait(v, "v", [T_["iota"]])
                T_["mask"] = sy.done(v.tensor_scalar(out=mask[:], in0=iidx[:],
                                                     scalar1=0.0,
                                                     scalar2=NEG_VIS,
                                                     op0=ALU.is_lt,
                                                     op1=ALU.mult), "v")
                T_["identf"] = sy.done(v.tensor_scalar(out=identf[:],
                                                       in0=iidx[:, 0:128],
                                                       scalar1=4.0, scalar2=None,
                                                       op0=ALU.is_equal), "v")
                T_["ones1"] = sy.done(v.memset(ones1[:], 1.0), "v")
                sy.wait(v, "v", [T_["identf"]])
                T_["identb"] = sy.done(v.tensor_copy(out=identb[:],
                                                     in_=identf[:]), "v")

            st2 = {"sA_free": [[], []], "sB_free": [[], []],
                   "e_free": [[], []],
                   "pT_free": [],
                   "psum_free": [None] * NPS, "zz_free": [], "rz_free": [],
                   "rrow_free": [], "psR_free": [], "pso_free": [],
                   "att_free": [], "psr_free": [], "nm_free": [],
                   "r8_free": [], "vstage_free": []}

            att_stores = []
            for hh in range(2):
                head_gate = [] if hh else [T_["cc1"]]

                @block.gpsimd
                def _(g, hh=hh, head_gate=head_gate):
                    sy.wait(g, "g", head_gate + st2.get("qkT_readers", []))
                    for j in range(NCORES):
                        b = j * 768 + hh * 128
                        tk = sy.done(g.dma_start(out=kT[:, j * T:(j + 1) * T],
                                                 in_=qkvo[b + 256:b + 384, :]),
                                     "ldhk", 16)
                        tq = sy.done(g.dma_start(out=qT[:, j * T:(j + 1) * T],
                                                 in_=qkvo[b:b + 128, :]),
                                     "ldhq", 16)
                    sy.wait(g, "g", st2["vstage_free"] + st2["sA_free"][0])
                    for j in range(NCORES):
                        b = j * 768 + hh * 128
                        tok = sy.done(g.dma_start(
                            out=sA2[0][:, j * T:(j + 1) * T],
                            in_=qkvo[b + 512:b + 640, :]), "ldhv", 16)
                    T_["hload_kq"] = [tk, tq]
                    T_["hload"] = tok

                @block.sync
                def _(s, hh=hh):
                    sy.wait(s, "s", [T_["hload"]] + st2["pT_free"])
                    tr = s.dma_start_transpose(
                        out=vt[:].rearrange("c (ct d) -> c ct d",
                                            d=128)[:, 0:S // 128, :],
                        in_=sA2[0][:, 0:S].rearrange("d (ct c) -> d ct c",
                                                     c=128))
                    T_["vtt"] = sy.done(tr, "dtv", 16)
                    st2["vstage_free"] = [T_["vtt"]]
                    st2["sA_free"][0] = st2["sA_free"][0] + [T_["vtt"]]

                @block.vector
                def _(v, hh=hh, head_gate=head_gate):
                    sy.wait(v, "v", head_gate + [T_.get("pk_stg_free")])
                    T_["pkz"] = sy.done(v.memset(pstg[:], 0.0), "v")

                @block.sync
                def _(s, hh=hh):
                    sy.wait(s, "s", [T_["pkz"]])
                    T_["pkld"] = sy.done(
                        s.dma_start(out=pstg[0:4, :],
                                    in_=ins["pk"][:, hh * 128:(hh + 1) * 128]),
                        "ldb", 16)

                @block.vector
                def _(v, hh=hh):
                    sy.wait(v, "v", [T_["pkld"], T_.get("pkt_prev")])
                    T_["pkb"] = sy.done(v.tensor_copy(out=pstgb[:],
                                                      in_=pstg[:]), "v")

                @block.tensor
                def _(te, hh=hh):
                    sy.wait(te, "t", [T_["pkb"], T_["identb"],
                                      T_.get("pkt_evac_prev")])
                    T_["pkt"] = sy.done(te.transpose(out=ps_t[:], in_=pstgb[:],
                                                     identity=identb[:]), "t")

                @block.vector
                def _(v, hh=hh):
                    sy.wait(v, "v", [T_["pkt"]] + T_["hload_kq"][:1])
                    T_["pk_in_kT"] = sy.done(v.tensor_copy(out=kT[:, S:S + 4],
                                                           in_=ps_t[:, 0:4]),
                                             "v")
                    T_["kT_zpad"] = sy.done(v.memset(kT[:, S + 4:S + 128], 0.0),
                                            "v")
                    sy.wait(v, "v", [T_["vtt"]])
                    T_["vt_z"] = sy.done(v.memset(vt[:, S:S + 128], 0.0), "v")
                    sy.wait(v, "v", [T_["pkt"]])
                    T_["pvz"] = sy.done(v.memset(pstg[:], 0.0), "v")
                T_["pkt_evac_prev"] = T_["pk_in_kT"]

                @block.sync
                def _(s, hh=hh):
                    sy.wait(s, "s", [T_["pvz"]])
                    T_["pvld"] = sy.done(
                        s.dma_start(out=pstg[0:4, :],
                                    in_=ins["pv"][:, hh * 128:(hh + 1) * 128]),
                        "ldb", 16)

                @block.vector
                def _(v, hh=hh):
                    sy.wait(v, "v", [T_["pvld"], T_["vt_z"]])
                    T_["pv_in_vt"] = sy.done(
                        v.tensor_copy(out=vt[0:4, S:S + 128],
                                      in_=pstg[0:4, :]), "v")
                T_["pk_stg_free"] = T_["pv_in_vt"]
                T_["pkt_prev"] = T_["pkb"]

                head_ready = T_["hload_kq"] + [T_["pk_in_kT"],
                                               T_["kT_zpad"]]
                pv_ready = [T_["vtt"], T_["vt_z"], T_["pv_in_vt"]]

                pend_epilogue = [None]

                def flush_epilogue():
                    if pend_epilogue[0] is not None:
                        fn = pend_epilogue[0]
                        pend_epilogue[0] = None
                        fn()

                tiles = []
                for sb in range(nsb):
                    qts = [2 * sb, 2 * sb + 1]
                    Wtrue = 128 * (qts[1] + 2)
                    W = min((Wtrue + 511) // 512 * 512, S + 128)
                    for j, qt in enumerate(qts):
                        tiles.append((sb, j, qt, W))

                def emit_scores_evac(idx):
                    sb, j, qt, W = tiles[idx]
                    pa = (sb * 2 + j) % 2
                    sA = sA2[pa]
                    nblk = (W + 511) // 512
                    mm_toks = []
                    ev_toks = []

                    @block.tensor
                    def _(te, qt=qt, nblk=nblk, W=W, mm_toks=mm_toks):
                        sy.wait(te, "t", head_ready)
                        sc_base = sy.counts["sc"]
                        for b in range(nblk):
                            c0, c1 = b * 512, min((b + 1) * 512, W)
                            if b < NPS:
                                sy.wait(te, "t", [st2["psum_free"][b % NPS]])
                            else:
                                sy.wait(te, "t", [("sc", sc_base + b - NPS + 1)])
                            mm = te.matmul(
                                out=ps_s[b % NPS][:, 0:c1 - c0],
                                lhsT=qT[:, qt * 128:(qt + 1) * 128],
                                rhs=kT[:, c0:c1],
                                start=True, stop=True)
                            mm_toks.append(sy.done(mm, "t"))
                            st2["qkT_readers"] = [mm_toks[-1]]

                    @block.scalar
                    def _(sc, nblk=nblk, W=W, sA=sA, pa=pa,
                          mm_toks=mm_toks, ev_toks=ev_toks):
                        sy.wait(sc, "sc", st2["sA_free"][pa])
                        for b in range(nblk):
                            c0, c1 = b * 512, min((b + 1) * 512, W)
                            sy.wait(sc, "sc", [mm_toks[b]])
                            ev = sc.activation(out=sA[:, c0:c1],
                                               in_=ps_s[b % NPS][:, 0:c1 - c0],
                                               func=AF.Copy, bias=0.0,
                                               scale=1.0)
                            tok = sy.done(ev, "sc")
                            st2["psum_free"][b % NPS] = tok
                            ev_toks.append(tok)
                    return mm_toks, ev_toks

                pend_se = emit_scores_evac(0)
                gate_toks = []
                tr_toks = []
                for idx in range(len(tiles)):
                    sb, j, qt, W = tiles[idx]
                    pa = (sb * 2 + j) % 2
                    sA = sA2[pa]
                    sB = sB2[pa]
                    e = e2[pa]
                    Wx = min(128 * (qt + 2), S + 128)
                    ntc = W // 128
                    if j == 0:
                        gate_toks = []
                        tr_toks = []
                    mm_toks, ev_toks = pend_se

                    @block.vector
                    def _(v, j=j, qt=qt, W=W, Wx=Wx, ev_toks=ev_toks,
                          sA=sA, pa=pa, e=e):
                        mw = Wx - 128 * qt
                        sy.wait(v, "v", ev_toks + [T_["mask"]])
                        tmask = sy.done(v.tensor_add(sA[:, Wx - mw:Wx],
                                                     sA[:, Wx - mw:Wx],
                                                     mask[:, 0:mw]), "v")
                        st2["sA_free"][pa] = []
                        if Wx < W:
                            sy.wait(v, "v", st2["e_free"][pa])
                            tz = sy.done(v.memset(e[:, Wx:W], 0.0), "v")
                            st2["e_free"][pa] = [tz]
                        sy.wait(v, "v", [tmask] + st2["r8_free"])
                        tmx = sy.done(v.max(out=r8[:, 0:8],
                                            in_=sA[:, 0:Wx]), "v")
                        sy.wait(v, "v", [tmx] + st2["nm_free"])
                        tnm = sy.done(v.tensor_scalar_mul(nm[:], r8[:, 0:1],
                                                          -1.0), "v")
                        sy.wait(v, "v", st2["sB_free"][pa])
                        tmr = sy.done(v.match_replace(
                            out=sB[:, 0:Wx], in_to_replace=r8[:, 0:8],
                            in_values=sA[:, 0:Wx], imm_value=SENT), "v")
                        for rnd in range(1, 8):
                            sy.wait(v, "v", [tmr])
                            tmx = sy.done(v.max(out=r8[:, 8 * rnd:8 * rnd + 8],
                                                in_=sB[:, 0:Wx]), "v")
                            sy.wait(v, "v", [tmx])
                            tmr = sy.done(v.match_replace(
                                out=sB[:, 0:Wx],
                                in_to_replace=r8[:, 8 * rnd:8 * rnd + 8],
                                in_values=sB[:, 0:Wx], imm_value=SENT), "v")
                        st2["sB_free"][pa] = []
                        st2["r8_free"] = [tmr]
                        T_["tnm"], T_["tmr8"] = tnm, tmr

                    if idx + 1 < len(tiles):
                        pend_se = emit_scores_evac(idx + 1)

                    @block.scalar
                    def _(sc, W=W, Wx=Wx, sA=sA, e=e, pa=pa):
                        sy.wait(sc, "sc", [T_["tnm"]] + st2["e_free"][pa])
                        ex = sc.activation(out=e[:, 0:Wx], in_=sA[:, 0:Wx],
                                           func=AF.Exp, bias=nm[:],
                                           scale=1.0)
                        T_["exp"] = sy.done(ex, "sc")
                        st2["e_free"][pa] = []
                        st2["nm_free"] = [T_["exp"]]
                        st2["sA_free"][pa] = [T_["exp"], T_["tmr8"]]

                    @block.vector
                    def _(v, j=j, W=W, Wx=Wx, e=e, sB=sB, pa=pa):
                        sy.wait(v, "v", [T_["exp"], T_["tmr8"]]
                                + st2["zz_free"])
                        g_ = v.scalar_tensor_tensor(
                            out=e[:, 0:Wx], in0=sB[:, 0:Wx],
                            scalar=SENT_CMP,
                            in1=e[:, 0:Wx], op0=ALU.is_lt, op1=ALU.mult,
                            accum_out=zz[:, j:j + 1])
                        T_["gate"] = sy.done(g_, "v")
                        gate_toks.append(T_["gate"])
                        st2["sB_free"][pa] = [T_["gate"]]

                    if j == 0:
                        flush_epilogue()

                    @block.sync
                    def _(s, j=j, W=W, ntc=ntc, e=e, pa=pa):
                        sy.wait(s, "s", [T_["gate"]] + st2["pT_free"])
                        tr = s.dma_start_transpose(
                            out=pT[:].rearrange("c (t j q) -> c t j q",
                                                j=2, q=128)[:, 0:ntc, j, :],
                            in_=e[:, 0:W].rearrange("q (t c) -> q t c",
                                                    c=128))
                        T_["tr"] = sy.done(tr, f"dt{pa}", 16)
                        tr_toks.append(T_["tr"])
                        st2["e_free"][pa] = [T_["tr"]]

                    if j != 1:
                        continue

                    @block.vector
                    def _(v, gate_toks=gate_toks):
                        sy.wait(v, "v", gate_toks + st2["rz_free"])
                        T_["rz"] = sy.done(v.reciprocal(out=rz[:], in_=zz[:]),
                                           "v")
                        st2["zz_free"] = [T_["rz"]]

                    def make_epilogue(hh=hh, sb=sb, ntc=ntc,
                                      tr_toks=tr_toks, rz_tok=None):
                        rz_tok = T_["rz"]

                        def emit():
                            for j in range(2):
                                @block.tensor
                                def _(te, j=j):
                                    sy.wait(te, "t", [rz_tok, T_["identf"]]
                                            + st2["psr_free"])
                                    T_[f"psr{j}"] = sy.done(
                                        te.transpose(out=ps_r[:],
                                                     in_=rz[:, j:j + 1],
                                                     identity=identf[:]), "t")

                                @block.scalar
                                def _(sc, j=j):
                                    sy.wait(sc, "sc", [T_[f"psr{j}"]]
                                            + st2["rrow_free"])
                                    ev = sc.activation(
                                        out=rrow[0:1, j * 128:(j + 1) * 128],
                                        in_=ps_r[:], func=AF.Copy, bias=0.0,
                                        scale=1.0)
                                    T_[f"rrow{j}"] = sy.done(ev, "sc")
                                    st2["psr_free"] = [T_[f"rrow{j}"]]

                            st2["rz_free"] = [T_["psr0"], T_["psr1"]]
                            st2["rrow_free"] = []

                            @block.tensor
                            def _(te):
                                sy.wait(te, "t", [T_["rrow0"], T_["rrow1"],
                                                  T_["ones1"]]
                                        + st2["psR_free"])
                                for j in range(2):
                                    mm = te.matmul(
                                        out=ps_R[:, j * 128:(j + 1) * 128],
                                        lhsT=ones1[:],
                                        rhs=rrow[0:1, j * 128:(j + 1) * 128],
                                        start=True, stop=True)
                                T_["Rmm"] = sy.done(mm, "t")
                                st2["rrow_free"] = [T_["Rmm"]]
                                sy.wait(te, "t", tr_toks + st2["pso_free"]
                                        + pv_ready)
                                for ct in range(ntc):
                                    mm = te.matmul(
                                        out=ps_o[:],
                                        lhsT=vt[:, ct * 128:(ct + 1) * 128],
                                        rhs=pT[:, ct * 256:(ct + 1) * 256],
                                        start=(ct == 0),
                                        stop=(ct == ntc - 1))
                                T_["pv"] = sy.done(mm, "t")
                                st2["pT_free"] = [T_["pv"]]
                            Rmm = T_["Rmm"]
                            pv = T_["pv"]

                            @block.vector
                            def _(v):
                                sy.wait(v, "v", [Rmm] + st2.get("Rt_free", []))
                                tR = sy.done(v.tensor_copy(out=Rt[:],
                                                           in_=ps_R[:]), "v")
                                st2["psR_free"] = [tR]
                                sy.wait(v, "v", [tR, pv] + st2["att_free"])
                                ta = sy.done(v.tensor_mul(att[:], ps_o[:],
                                                          Rt[:]), "v")
                                T_["att"] = ta
                                st2["pso_free"] = [ta]
                                st2["Rt_free"] = [ta]

                            @block.sync
                            def _(s):
                                sy.wait(s, "s", [T_["att"]])
                                spr = T // 256
                                r, c0 = sb // spr, (sb % spr) * 256
                                stt = s.dma_start(
                                    out=ati[r * 256 + hh * 128:
                                            r * 256 + (hh + 1) * 128,
                                            c0:c0 + 256],
                                    in_=att[:])
                                st2_t = sy.done(stt, "sta", 16)
                                T_["store"] = st2_t
                                att_stores.append(st2_t)
                                st2["att_free"] = [st2_t]
                        return emit

                    pend_epilogue[0] = make_epilogue()

                flush_epilogue()
                T_["headdone"] = T_["store"]

            @block.gpsimd
            def _(g):
                wo_toks = []
                for i in range(HC):
                    wo_toks.append(sy.done(
                        g.dma_start(out=wobf[i * 128:(i + 1) * 128, :],
                                    in_=ins["Wo"][i * 128:(i + 1) * 128, :]),
                        f"ldg{i % 2}", 16))
                T_["wocast"] = wo_toks[-2:]

            @block.gpsimd
            def _(g):
                sy.wait(g, "g", att_stores)
                cc = g.collective_compute("AllToAll", ALU.bypass,
                                          replica_groups=RG,
                                          ins=[ati.opt()], outs=[ato.opt()])
                T_["cc2"] = sy.done(cc, "cc", 1)

        sy.set_floor()
        with ExitStack() as ctx:
            ec = ctx.enter_context
            aT = ec(nc.sbuf_tensor("aTs", [128, HC * T], BF16))
            Wbf3 = ec(nc.sbuf_tensor("Wbf3", [128, HC * E], BF16))

            bstg3 = ec(nc.sbuf_tensor("bstg3", [1, E], F32))
            bob = ec(nc.sbuf_tensor("bob", [1, E], BF16))
            ones3 = ec(nc.sbuf_tensor("ones3", [1, 128], BF16))
            pstg3 = [ec(nc.sbuf_tensor(f"pstg3{i}", [128, E], F32))
                     for i in range(2)]
            outf3 = [ec(nc.sbuf_tensor(f"outf3{i}", [128, E], F32))
                     for i in range(2)]
            NPS = 4
            psl3 = [ec(nc.psum_tensor(f"psl3{i}", [128, 512], F32))
                    for i in range(NPS)]

            st3 = {"psum_free": [None] * NPS,
                   "pstg_free": [[], []], "outf_free": [[], []]}

            @block.vector
            def _(v):
                sy.wait(v, "v", [])

            @block.scalar
            def _(sc):
                sy.wait(sc, "sc", [])

            @block.tensor
            def _(te):
                sy.wait(te, "t", [])

            @block.gpsimd
            def _(g):
                sy.wait(g, "g", [])

            @block.sync
            def _(s):
                sy.wait(s, "s", [])

            @block.vector
            def _(v):
                T_["ones3"] = sy.done(v.memset(ones3[:], 1.0), "v")

            @block.gpsimd
            def _(g):
                sy.wait(g, "g", [T_["cc2"]])
                for i in range(4):
                    tok = sy.done(g.dma_start(
                        out=aT[:].rearrange("c (hc t) -> c hc t",
                                            t=T)[:, i * 4:(i + 1) * 4, :],
                        in_=ato.rearrange("(hc c) t -> c hc t",
                                          c=128)[:, i * 4:(i + 1) * 4, :]),
                        "lda", 16)
                T_["ald"] = tok

            @block.sync
            def _(s):
                sy.wait(s, "s", [T_["pstore"]])
                T_["bld3"] = sy.done(s.dma_start(out=bstg3[:], in_=ins["bo"]),
                                     "ldb", 16)

            @block.vector
            def _(v):
                sy.wait(v, "v", [T_["bld3"]])
                T_["bo"] = sy.done(v.tensor_copy(out=bob[:], in_=bstg3[:]), "v")

            wls = []
            for i in range(HC):
                if i % 2 == 0:
                    @block.sync
                    def _(s, i=i):
                        if i == 0:
                            sy.wait(s, "s", T_["wocast"])
                        wls.append(sy.done(
                            s.dma_start(out=Wbf3[:, i * E:(i + 1) * E],
                                        in_=wobf[i * 128:(i + 1) * 128, :]),
                            "ldw0", 16))
                else:
                    @block.scalar
                    def _(sc, i=i):
                        if i == 1:
                            sy.wait(sc, "sc", T_["wocast"])
                        wls.append(sy.done(
                            sc.dma_start(out=Wbf3[:, i * E:(i + 1) * E],
                                         in_=wobf[i * 128:(i + 1) * 128, :]),
                            "ldw1", 16))

            for tt in range(T // 128):
                p = tt % 2

                @block.scalar
                def _(sc, tt=tt, p=p):
                    sy.wait(sc, "sc", st3["pstg_free"][p])
                    T_["pld"] = sy.done(
                        sc.dma_start(out=pstg3[p][:],
                                     in_=part[tt * 128:(tt + 1) * 128, :]),
                        f"lp{p}", 16)

                cmb = []
                for jb in range(E // 512):
                    bank = (tt * 4 + jb) % NPS

                    @block.tensor
                    def _(te, tt=tt, jb=jb, bank=bank):
                        sy.wait(te, "t", wls + [T_["ald"], T_["bo"], T_["ones3"],
                                                st3["psum_free"][bank]])
                        for hc in range(HC):
                            te.matmul(
                                out=psl3[bank][:],
                                lhsT=aT[:, hc * T + tt * 128:
                                        hc * T + (tt + 1) * 128],
                                rhs=Wbf3[:, hc * E + jb * 512:
                                         hc * E + (jb + 1) * 512],
                                start=(hc == 0), stop=False)
                        mm = te.matmul(out=psl3[bank][:], lhsT=ones3[:],
                                       rhs=bob[:, jb * 512:(jb + 1) * 512],
                                       start=False, stop=True)
                        T_["omm"] = sy.done(mm, "t")

                    @block.vector
                    def _(v, tt=tt, jb=jb, bank=bank, p=p):
                        sy.wait(v, "v", [T_["omm"], T_["pld"]]
                                + st3["outf_free"][p])
                        o1 = sy.done(v.tensor_add(
                            outf3[p][:, jb * 512:(jb + 1) * 512], psl3[bank][:],
                            pstg3[p][:, jb * 512:(jb + 1) * 512]), "v")
                        st3["psum_free"][bank] = o1
                        cmb.append(o1)

                @block.sync
                def _(s, tt=tt, p=p, cmb=cmb):
                    sy.wait(s, "s", cmb)
                    stt = s.dma_start(out=out_ap[tt * 128:(tt + 1) * 128, :],
                                      in_=outf3[p][:])
                    tok = sy.done(stt, f"st{p}", 16)
                    st3["outf_free"][p] = [tok]
                    st3["pstg_free"][p] = [tok] + cmb
                    T_[f"ostore{p}"] = tok

            @block.sync
            def _(s):
                sy.wait(s, "s", [T_["ostore0"], T_["ostore1"]])

    sy.close()
    return nc


S_FULL = 8192
E_FULL = 2048
NCORES = 8


def kernel(L0_attn_x, L0_attn_Wq, L0_attn_bq, L0_attn_Wk, L0_attn_bk,
           L0_attn_Wv, L0_attn_bv, L0_attn_Wo, L0_attn_bo, L0_attn_pk,
           L0_attn_pv, L0_ffn_A, L0_ffn_B, L0_ffn_W, L0_ffn_bias):
    S, E = S_FULL, E_FULL
    Tl = S // NCORES
    x = np.ascontiguousarray(np.asarray(L0_attn_x, dtype=np.float32)
                             .reshape(S, E))
    A2T = np.ascontiguousarray(
        np.asarray(L0_ffn_A, dtype=np.float32).reshape(1024, E).T)
    B2 = np.ascontiguousarray(
        np.asarray(L0_ffn_B, dtype=np.float32).transpose(0, 2, 1).reshape(1024, E))
    WT = np.ascontiguousarray(np.asarray(L0_ffn_W, dtype=np.float32).T)
    Wq = np.ascontiguousarray(np.asarray(L0_attn_Wq, dtype=np.float32))
    Wk = np.ascontiguousarray(np.asarray(L0_attn_Wk, dtype=np.float32))
    Wv = np.ascontiguousarray(np.asarray(L0_attn_Wv, dtype=np.float32))
    Wo = np.ascontiguousarray(np.asarray(L0_attn_Wo, dtype=np.float32))
    bs = {"bq": L0_attn_bq, "bk": L0_attn_bk, "bv": L0_attn_bv,
          "fb": L0_ffn_bias, "bo": L0_attn_bo}
    bs = {k: np.ascontiguousarray(np.asarray(v, dtype=np.float32)
                                  .reshape(1, E)) for k, v in bs.items()}
    pk = np.ascontiguousarray(np.asarray(L0_attn_pk, dtype=np.float32)
                              .reshape(4, E))
    pv = np.ascontiguousarray(np.asarray(L0_attn_pv, dtype=np.float32)
                              .reshape(4, E))

    nc = build_fused(S, NCORES)
    maps = []
    for c in range(NCORES):
        m = {"x": np.ascontiguousarray(x[c * Tl:(c + 1) * Tl]),
             "Wq": Wq, "Wk": Wk, "Wv": Wv, "WT": WT, "Wo": Wo,
             "A2T": A2T, "B2": B2,
             "pk": np.ascontiguousarray(pk[:, c * 256:(c + 1) * 256]),
             "pv": np.ascontiguousarray(pv[:, c * 256:(c + 1) * 256])}
        m.update(bs)
        maps.append(m)
    res = run_bass_kernel_spmd(nc, maps, core_ids=list(range(NCORES)))
    out = np.concatenate([np.asarray(res.results[c]["out"])
                          for c in range(NCORES)], axis=0)
    return out.reshape(1, S, E).astype(np.float32)


# version 27 (final)
# speedup vs baseline: 1.0078x; 1.0030x over previous
import os
os.environ.setdefault("TRN_ALLOW_ICI_ALL_TO_ALL", "1")
import numpy as np
import concourse.bass as bass
import concourse.mybir as mybir
from concourse.bass_utils import run_bass_kernel_spmd

F32 = mybir.dt.float32
BF16 = mybir.dt.bfloat16
AF = mybir.ActivationFunctionType
ALU = mybir.AluOpType
AX = mybir.AxisListType

NEG_VIS = -1.0e30
SENT = -3.0e38
SENT_CMP = -2.0e38


class Syncer:
    def __init__(self, nc):
        self.nc = nc
        self._ctxs = []
        self.sems = {}
        self.counts = {}
        self.waited = {}
        self.floor = {}

    def sem(self, name):
        if name not in self.sems:
            cm = self.nc.semaphore(name)
            s = cm.__enter__()
            self._ctxs.append(cm)
            self.sems[name] = s
            self.counts[name] = 0
        return self.sems[name]

    def close(self):
        for cm in reversed(self._ctxs):
            cm.__exit__(None, None, None)

    DMA_PREFIX = ("ld", "dt", "st", "xt", "gt", "cc", "lp", "lda")

    def set_floor(self):
        self.floor = {k: v for k, v in self.counts.items() if v > 0}

    def wait(self, eng, eng_name, toks):
        need = dict(self.floor)
        for t in toks:
            if t is None:
                continue
            sem_name, val = t
            if sem_name.startswith(self.DMA_PREFIX):
                val = self.counts[sem_name]
            if val > need.get(sem_name, -1):
                need[sem_name] = val
        w = self.waited.setdefault(eng_name, {})
        for sem_name, val in sorted(need.items()):
            if w.get(sem_name, 0) >= val:
                continue
            eng.wait_ge(self.sems[sem_name], val)
            w[sem_name] = val

    def done(self, inst, sem_name, n=1):
        self.sem(sem_name)
        self.counts[sem_name] += n
        v = self.counts[sem_name]
        inst.then_inc(self.sems[sem_name], n)
        return (sem_name, v)


def build_fused(S, NCORES=8):
    from contextlib import ExitStack
    T = S // NCORES
    E = 2048
    HC = E // 128
    NTT = T // 128
    TBW = min(512, T)
    NTB = T // TBW
    AR = 1024
    SC_Q = float(1.0 / np.sqrt(128.0))
    nq = S // 128
    nsb = nq // 2
    WMAX = S + 128
    NTMAX = WMAX // 128
    RG = [list(range(NCORES))]

    nc = bass.Bass("TRN2", target_bir_lowering=False)
    sy = Syncer(nc)
    for s_ in ("v", "t", "sc", "g", "ld0", "ld1", "ldw0", "ldw1", "ldg0", "ldg1", "ldb", "xt", "gt",
               "stq", "stp", "sta", "st0", "st1", "cc", "ldh", "ldhk", "ldhq",
               "ldhv", "ldw2", "ldw3", "ldA0", "ldA1", "lds0", "lds1", "dt0", "dt1", "dtv",
               "lp0", "lp1", "lda"):
        sy.sem(s_)

    ins = {}
    ins["x"] = nc.dram_tensor("x", [T, E], F32, kind="ExternalInput").ap()
    for nmw in ("Wq", "Wk", "Wv", "WT", "Wo"):
        ins[nmw] = nc.dram_tensor(nmw, [E, E], F32, kind="ExternalInput").ap()
    ins["A2T"] = nc.dram_tensor("A2T", [E, AR], F32, kind="ExternalInput").ap()
    ins["B2"] = nc.dram_tensor("B2", [AR, E], F32, kind="ExternalInput").ap()
    for nmb in ("bq", "bk", "bv", "fb", "bo"):
        ins[nmb] = nc.dram_tensor(nmb, [1, E], F32, kind="ExternalInput").ap()
    ins["pk"] = nc.dram_tensor("pk", [4, 256], F32, kind="ExternalInput").ap()
    ins["pv"] = nc.dram_tensor("pv", [4, 256], F32, kind="ExternalInput").ap()
    out_ap = nc.dram_tensor("out", [T, E], F32, kind="ExternalOutput").ap()

    qkvi = nc.dram_tensor("qkvi", [NCORES * 768, T], BF16, kind="Internal").ap()
    qkvo = nc.dram_tensor("qkvo", [NCORES * 768, T], BF16, kind="Internal").ap()
    ati = nc.dram_tensor("ati", [NCORES * 256, T], BF16, kind="Internal").ap()
    wobf = nc.dram_tensor("wobf", [E, E], BF16, kind="Internal").ap()
    ato = nc.dram_tensor("ato", [NCORES * 256, T], BF16, kind="Internal").ap()
    part = nc.dram_tensor("part", [T, E], F32, kind="Internal").ap()

    with ExitStack() as octx:
        block = octx.enter_context(nc.Block())
        T_ = {}

        with ExitStack() as ctx:
            ec = ctx.enter_context
            xstg = [ec(nc.sbuf_tensor(f"xstg{i}", [128, E], F32))
                    for i in range(2)]
            xn = ec(nc.sbuf_tensor("xn", [128, E], BF16))
            xnT = ec(nc.sbuf_tensor("xnT", [128, HC * T], BF16,
                                    is_transpose_dest=True))
            Wbf = ec(nc.sbuf_tensor("Wbf", [128, HC * E], BF16))

            bstg = ec(nc.sbuf_tensor("bstg", [1, E], F32))
            bias_b = {nmb: ec(nc.sbuf_tensor(f"bias_{nmb}", [1, E], BF16))
                      for nmb in ("bq", "bk", "bv", "fb")}
            ones = ec(nc.sbuf_tensor("ones", [1, 512], BF16))
            ostg = [ec(nc.sbuf_tensor("ostg0", [128, T], BF16))]
            vecs = ec(nc.sbuf_tensor("vecs", [128, NTT * AR], BF16,
                                     is_transpose_dest=True))
            arr = ec(nc.sbuf_tensor("arr", [128, 64], F32))
            m8 = ec(nc.sbuf_tensor("m8", [128, 8], F32))
            routes = ec(nc.sbuf_tensor("routes", [128, 64], BF16))
            tmp = ec(nc.sbuf_tensor("tmpb", [128, AR], BF16))
            vsel = ec(nc.sbuf_tensor("vsel", [128, 16], F32))
            G = ec(nc.sbuf_tensor("G", [128, AR], BF16))
            delta = ec(nc.sbuf_tensor("delta", [128, HC * AR], BF16))
            wsg = [ec(nc.sbuf_tensor(f"wsg{i}", [128, AR], F32))
                   for i in range(2)]
            s1 = ec(nc.sbuf_tensor("s1", [128, 1], F32))
            s2 = ec(nc.sbuf_tensor("s2", [128, 1], F32))
            mu = ec(nc.sbuf_tensor("mu", [128, 1], F32))
            t2 = ec(nc.sbuf_tensor("t2", [128, 1], F32))
            varb = ec(nc.sbuf_tensor("varb", [128, 1], F32))
            rstd = ec(nc.sbuf_tensor("rstd", [128, 1], F32))
            nmur = ec(nc.sbuf_tensor("nmur", [128, 1], F32))
            NPS = 4
            psl = [ec(nc.psum_tensor(f"psl{i}", [128, 512], F32))
                   for i in range(8)]

            st_ = {"psum_free": [None] * 8, "xstg_free": [[], []],
                   "ostg_free": [[]],
                   "xn_free": [], "sqs_free": [], "stats_free": [],
                   "a2t_readers": [], "wsg_free": [[], []],
                   "tmp_free": [], "G_free": [], "outf_free": [],
                   "bstg_free": [], "Wbf_readers": []}

            @block.vector
            def _(v):
                T_["ones"] = sy.done(v.memset(ones[:], 1.0), "v")

            for tt in range(NTT):
                p = tt % 2

                @block.sync
                def _(s, tt=tt, p=p):
                    sy.wait(s, "s", st_["xstg_free"][p])
                    T_["xld"] = sy.done(
                        s.dma_start(out=xstg[p][:],
                                    in_=ins["x"][tt * 128:(tt + 1) * 128, :]),
                        f"ld{p}", 16)

                @block.vector
                def _(v, tt=tt, p=p):
                    sy.wait(v, "v", [T_["xld"]] + st_["stats_free"])
                    T_["s1"] = sy.done(v.reduce_sum(out=s1[:], in_=xstg[p][:],
                                                    axis=AX.X), "v")

                @block.scalar
                def _(sc, tt=tt, p=p):
                    sy.wait(sc, "sc", [T_["xld"]] + st_["xn_free"]
                            + st_["stats_free"])
                    T_["s2"] = sy.done(sc.activation(out=xn[:], in_=xstg[p][:],
                                                     func=AF.Square, bias=0.0,
                                                     scale=1.0,
                                                     accum_out=s2[:]), "sc")

                @block.vector
                def _(v, tt=tt, p=p):
                    sy.wait(v, "v", [T_["s1"], T_["s2"]])
                    a1 = sy.done(v.tensor_scalar_mul(mu[:], s1[:], 1.0 / E), "v")
                    sy.wait(v, "v", [a1])
                    a2 = sy.done(v.tensor_mul(t2[:], mu[:], mu[:]), "v")
                    a3 = sy.done(v.tensor_scalar_mul(varb[:], s2[:], 1.0 / E), "v")
                    sy.wait(v, "v", [a2, a3])
                    a4 = sy.done(v.tensor_sub(varb[:], varb[:], t2[:]), "v")
                    sy.wait(v, "v", [a4])
                    a5 = sy.done(v.tensor_scalar_add(varb[:], varb[:], 1e-5), "v")
                    sy.wait(v, "v", [a5])
                    T_["rcp"] = sy.done(v.reciprocal(out=rstd[:], in_=varb[:]), "v")

                @block.scalar
                def _(sc, tt=tt, p=p):
                    sy.wait(sc, "sc", [T_["rcp"]])
                    b1 = sy.done(sc.activation(out=rstd[:], in_=rstd[:],
                                               func=AF.Sqrt, bias=0.0,
                                               scale=1.0), "sc")
                    sy.wait(sc, "sc", [b1])
                    T_["nmr"] = sy.done(sc.activation(out=nmur[:], in_=mu[:],
                                                      func=AF.Identity, bias=0.0,
                                                      scale=rstd[:]), "sc")

                @block.vector
                def _(v, tt=tt, p=p):
                    sy.wait(v, "v", [T_["nmr"]])
                    T_["nmn"] = sy.done(v.tensor_scalar_mul(nmur[:], nmur[:],
                                                            -1.0), "v")

                @block.scalar
                def _(sc, tt=tt, p=p):
                    sy.wait(sc, "sc", [T_["nmn"], T_["s2"]] + st_["xn_free"])
                    T_["xn"] = sy.done(sc.activation(out=xn[:], in_=xstg[p][:],
                                                     func=AF.Identity,
                                                     bias=nmur[:],
                                                     scale=rstd[:]), "sc")
                    st_["xstg_free"][p] = [T_["xn"], T_["s1"], T_["s2"]]
                    st_["stats_free"] = [T_["xn"]]

                @block.sync
                def _(s, tt=tt):
                    sy.wait(s, "s", [T_["xn"]])
                    xv = xnT[:].rearrange("c (hc t) -> c hc t", t=T)
                    tr = s.dma_start_transpose(
                        out=xv[:, :, tt * 128:(tt + 1) * 128],
                        in_=xn[:].rearrange("q (hc c) -> q hc c", c=128))
                    T_["xnT"] = sy.done(tr, "xt", 16)
                    st_["xn_free"] = [T_["xnT"]]

            xnT_tok = T_["xnT"]

            def load_weights(src_ap, nchunk, width, dst=None, rkey="Wbf_readers",
                             sem="ldg"):
                if dst is None:
                    dst = Wbf
                prev_readers = st_[rkey]
                st_[rkey] = []
                toks = []
                for i in range(nchunk):
                    p = i % 2

                    @block.gpsimd
                    def _(g, i=i, p=p):
                        sy.wait(g, "g", prev_readers)
                        toks.append(sy.done(
                            g.dma_start(
                                out=dst[:, i * width:(i + 1) * width],
                                in_=src_ap[i * 128:(i + 1) * 128, :]),
                            f"{sem}{p}", 16))
                return toks

            def staged_chunk(src_rows, dst_cols, k, prev_readers,
                             on_sync=False):
                p = k % 2

                if on_sync:
                    @block.sync
                    def _(s, p=p):
                        sy.wait(s, "s", st_["wsg_free"][p])
                        T_["wsld"] = sy.done(
                            s.dma_start(out=wsg[p][:], in_=src_rows),
                            f"lds{p}", 16)
                else:
                    @block.scalar
                    def _(sc, p=p):
                        sy.wait(sc, "sc", st_["wsg_free"][p])
                        T_["wsld"] = sy.done(
                            sc.dma_start(out=wsg[p][:], in_=src_rows),
                            f"lds{p}", 16)

                out_tok = []

                @block.vector
                def _(v, p=p):
                    sy.wait(v, "v", [T_["wsld"]] + prev_readers)
                    ck = sy.done(v.tensor_copy(out=dst_cols, in_=wsg[p][:]),
                                 "v")
                    st_["wsg_free"][p] = [ck]
                    out_tok.append(ck)
                return out_tok[0]

            def load_bias(name):
                @block.sync
                def _(s, name=name):
                    sy.wait(s, "s", st_["bstg_free"])
                    T_["bld"] = sy.done(s.dma_start(out=bstg[:], in_=ins[name]),
                                        "ldb", 16)

                @block.vector
                def _(v, name=name):
                    sy.wait(v, "v", [T_["bld"]])
                    tok = sy.done(v.tensor_copy(out=bias_b[name][:],
                                                in_=bstg[:]), "v")
                    st_["bstg_free"] = [tok]
                    T_[f"b_{name}"] = tok

            wtA = []
            for i in range(HC):
                wtA.append(staged_chunk(
                    ins["A2T"][i * 128:(i + 1) * 128, :],
                    delta[:, i * AR:(i + 1) * AR], i, []))
            st_["a2t_readers"] = []

            qkv_stores = []
            for pi, (wname, bname, off, scale) in enumerate(
                    [("Wq", "bq", 0, SC_Q), ("Wk", "bk", 256, 1.0),
                     ("Wv", "bv", 512, 1.0)]):
                wt = load_weights(ins[wname], HC, E)
                load_bias(bname)
                for lt in range(HC):
                    po = 0
                    ev_toks = []
                    for tb in range(NTB):
                        bank = (lt * NTB + tb) % NPS

                        @block.tensor
                        def _(te, lt=lt, tb=tb, bank=bank, bname=bname, wt=wt):
                            sy.wait(te, "t", wt + [T_[f"b_{bname}"], T_["ones"],
                                                   xnT_tok,
                                                   st_["psum_free"][bank]])
                            for hc in range(HC):
                                te.matmul(
                                    out=psl[bank][:, 0:TBW],
                                    lhsT=Wbf[:, hc * E + lt * 128:
                                             hc * E + (lt + 1) * 128],
                                    rhs=xnT[:, hc * T + tb * TBW:
                                            hc * T + (tb + 1) * TBW],
                                    start=(hc == 0), stop=False)
                            mm = te.matmul(
                                out=psl[bank][:, 0:TBW],
                                lhsT=bias_b[bname][:, lt * 128:(lt + 1) * 128],
                                rhs=ones[:, 0:TBW], start=False, stop=True)
                            T_["pmm"] = sy.done(mm, "t")
                            st_["Wbf_readers"].append(T_["pmm"])

                        @block.scalar
                        def _(sc, lt=lt, tb=tb, bank=bank, po=po, scale=scale):
                            sy.wait(sc, "sc", [T_["pmm"]] + st_["ostg_free"][po])
                            ev = sc.activation(
                                out=ostg[po][:, tb * TBW:(tb + 1) * TBW],
                                in_=psl[bank][:, 0:TBW], func=AF.Copy,
                                bias=0.0, scale=scale)
                            tok = sy.done(ev, "sc")
                            st_["psum_free"][bank] = tok
                            ev_toks.append(tok)

                    @block.sync
                    def _(s, lt=lt, po=po, off=off, ev_toks=ev_toks):
                        sy.wait(s, "s", ev_toks)
                        r, hh = lt // 2, lt % 2
                        base = r * 768 + off + hh * 128
                        stt = s.dma_start(out=qkvi[base:base + 128, :],
                                          in_=ostg[po][:, 0:T])
                        tok = sy.done(stt, "stq", 16)
                        st_["ostg_free"][po] = [tok]
                        qkv_stores.append(tok)

            @block.gpsimd
            def _(g):
                sy.wait(g, "g", qkv_stores)
                cc = g.collective_compute("AllToAll", ALU.bypass,
                                          replica_groups=RG,
                                          ins=[qkvi.opt()], outs=[qkvo.opt()])
                T_["cc1"] = sy.done(cc, "cc", 1)

            vec_toks = []
            for tt in range(NTT):
                for tb in range(AR // 512):
                    bank = (tt * 2 + tb) % NPS

                    @block.tensor
                    def _(te, tt=tt, tb=tb, bank=bank, wtA=wtA):
                        sy.wait(te, "t", wtA + [xnT_tok, st_["psum_free"][bank]])
                        for hc in range(HC):
                            mm = te.matmul(
                                out=psl[bank][:],
                                lhsT=xnT[:, hc * T + tt * 128:
                                         hc * T + (tt + 1) * 128],
                                rhs=delta[:, hc * AR + tb * 512:
                                          hc * AR + (tb + 1) * 512],
                                start=(hc == 0), stop=(hc == HC - 1))
                        T_["vmm"] = sy.done(mm, "t")
                        st_["a2t_readers"].append(T_["vmm"])

                    @block.scalar
                    def _(sc, tt=tt, tb=tb, bank=bank):
                        sy.wait(sc, "sc", [T_["vmm"]])
                        ev = sc.activation(out=vecs[:, tt * AR + tb * 512:
                                                    tt * AR + (tb + 1) * 512],
                                           in_=psl[bank][:], func=AF.Copy,
                                           bias=0.0, scale=1.0)
                        tok = sy.done(ev, "sc")
                        st_["psum_free"][bank] = tok
                        vec_toks.append(tok)

            wt = load_weights(ins["WT"], HC, E)
            load_bias("fb")
            wtB = []

            def emit_b2(k):
                rc, h = k // 2, k % 2
                wtB.append(staged_chunk(
                    ins["B2"][rc * 128:(rc + 1) * 128,
                              h * AR:(h + 1) * AR],
                    delta[:, rc * E + h * AR:rc * E + (h + 1) * AR], k,
                    st_["a2t_readers"], on_sync=True))

            for tt in range(NTT):
                vslice = vecs[:, tt * AR:(tt + 1) * AR]

                @block.scalar
                def _(sc, tt=tt, vslice=vslice):
                    sy.wait(sc, "sc", vec_toks[2 * tt:2 * tt + 2]
                            + st_["xn_free"] + st_["sqs_free"])
                    T_["sq"] = sy.done(sc.activation(out=xn[:, 0:AR], in_=vslice,
                                                     func=AF.Square, bias=0.0,
                                                     scale=1.0), "sc")

                @block.vector
                def _(v, tt=tt, vslice=vslice):
                    sy.wait(v, "v", [T_["sq"]])
                    c1 = sy.done(v.reduce_sum(
                        out=arr[:],
                        in_=xn[:, 0:AR].rearrange("p (a r) -> p a r", r=16),
                        axis=AX.X), "v")
                    sy.wait(v, "v", [c1])
                    c2 = sy.done(v.max(out=m8[:], in_=arr[:]), "v")
                    sy.wait(v, "v", [c2])
                    c3 = sy.done(v.tensor_scalar(out=routes[:], in0=arr[:],
                                                 scalar1=m8[:, 3:4], scalar2=0.25,
                                                 op0=ALU.is_ge,
                                                 op1=ALU.mult), "v")
                    sy.wait(v, "v", [c3] + st_["tmp_free"])
                    rb = routes[:].rearrange("p (a o) -> p a o", o=1) \
                        .to_broadcast([128, 64, 16])
                    c4 = sy.done(v.tensor_tensor(
                        out=tmp[:].rearrange("p (a r) -> p a r", r=16),
                        in0=vslice.rearrange("p (a r) -> p a r", r=16),
                        in1=rb, op=ALU.mult), "v")
                    sy.wait(v, "v", [c4])
                    c5 = sy.done(v.reduce_sum(
                        out=vsel[:],
                        in_=tmp[:].rearrange("p (a r) -> p r a", r=16),
                        axis=AX.X), "v")
                    sy.wait(v, "v", [c5] + st_["G_free"])
                    vb2 = vsel[:].rearrange("p (o r) -> p o r", o=1) \
                        .to_broadcast([128, 64, 16])
                    rb2 = routes[:].rearrange("p (a o) -> p a o", o=1) \
                        .to_broadcast([128, 64, 16])
                    T_["G"] = sy.done(v.tensor_tensor(
                        out=G[:].rearrange("p (a r) -> p a r", r=16),
                        in0=rb2, in1=vb2, op=ALU.mult), "v")
                    st_["sqs_free"] = [c1]
                    st_["xn_free"] = [c1]
                    st_["tmp_free"] = [c5]

                for k in range(tt * 16 // NTT, (tt + 1) * 16 // NTT):
                    emit_b2(k)

                @block.sync
                def _(s, tt=tt):
                    sy.wait(s, "s", [T_["G"]])
                    tr = s.dma_start_transpose(
                        out=vecs[:, tt * AR:(tt + 1) * AR]
                        .rearrange("c (rc q) -> c rc q", q=128),
                        in_=G[:].rearrange("q (rc c) -> q rc c", c=128))
                    T_[f"GTt{tt}"] = sy.done(tr, "gt", 16)
                    st_["G_free"] = [T_[f"GTt{tt}"]]

            def emit_base(tt):
                p = tt % 2

                @block.sync
                def _(s, tt=tt, p=p):
                    sy.wait(s, "s", st_["xstg_free"][p])
                    T_[f"xld2_{tt}"] = sy.done(
                        s.dma_start(out=xstg[p][:],
                                    in_=ins["x"][tt * 128:(tt + 1) * 128, :]),
                        f"ld{p}", 16)

                @block.tensor
                def _(te, tt=tt):
                    sy.wait(te, "t", wt + [T_["b_fb"], T_["ones"]]
                            + [st_["psum_free"][b] for b in range(4)])
                    for jb in range(E // 512):
                        for hc in range(HC):
                            te.matmul(
                                out=psl[jb][:],
                                lhsT=xnT[:, hc * T + tt * 128:
                                         hc * T + (tt + 1) * 128],
                                rhs=Wbf[:, hc * E + jb * 512:
                                        hc * E + (jb + 1) * 512],
                                start=(hc == 0), stop=False)
                        mm = te.matmul(out=psl[jb][:], lhsT=ones[:, 0:128],
                                       rhs=bias_b["fb"][:, jb * 512:
                                                        (jb + 1) * 512],
                                       start=False, stop=True)
                    T_[f"bmm{tt}"] = sy.done(mm, "t")
                    st_["Wbf_readers"].append(T_[f"bmm{tt}"])

                cmb = []
                for jb in range(E // 512):

                    @block.vector
                    def _(v, tt=tt, jb=jb, p=p):
                        sy.wait(v, "v", [T_[f"bmm{tt}"], T_[f"xld2_{tt}"]])
                        sl = xstg[p][:, jb * 512:(jb + 1) * 512]
                        o1 = sy.done(v.tensor_add(sl, sl, psl[jb][:]), "v")
                        st_["psum_free"][jb] = o1
                        cmb.append(o1)
                return cmb

            def emit_delta(tt, cmb):
                p = tt % 2

                @block.tensor
                def _(te, tt=tt):
                    sy.wait(te, "t", wtB + [T_[f"GTt{tt}"]]
                            + [st_["psum_free"][4 + b] for b in range(4)])
                    for jb in range(E // 512):
                        for rc in range(AR // 128):
                            mm = te.matmul(
                                out=psl[4 + jb][:],
                                lhsT=vecs[:, tt * AR + rc * 128:
                                          tt * AR + (rc + 1) * 128],
                                rhs=delta[:, rc * E + jb * 512:
                                          rc * E + (jb + 1) * 512],
                                start=(rc == 0), stop=(rc == AR // 128 - 1))
                    T_[f"dmm{tt}"] = sy.done(mm, "t")

                cmb2 = []
                for jb in range(E // 512):

                    @block.vector
                    def _(v, tt=tt, jb=jb, p=p):
                        sy.wait(v, "v", [T_[f"dmm{tt}"]] + cmb)
                        sl = xstg[p][:, jb * 512:(jb + 1) * 512]
                        o2 = sy.done(v.tensor_add(sl, sl, psl[4 + jb][:]), "v")
                        st_["psum_free"][4 + jb] = o2
                        cmb2.append(o2)

                @block.sync
                def _(s, tt=tt, p=p, cmb2=cmb2):
                    sy.wait(s, "s", cmb2)
                    stt = s.dma_start(out=part[tt * 128:(tt + 1) * 128, :],
                                      in_=xstg[p][:])
                    tok = sy.done(stt, "stp", 16)
                    st_["xstg_free"][p] = [tok] + cmb2
                    T_["pstore"] = tok

            prev = None
            for tt in range(NTT):
                cmb = emit_base(tt)
                if prev is not None:
                    emit_delta(prev[0], prev[1])
                prev = (tt, cmb)
            emit_delta(prev[0], prev[1])

        sy.set_floor()
        with ExitStack() as ctx:
            ec = ctx.enter_context
            qT = ec(nc.sbuf_tensor("qT", [128, S], BF16))
            kT = ec(nc.sbuf_tensor("kTp", [128, WMAX], BF16))
            vt = ec(nc.sbuf_tensor("vt", [128, NTMAX * 128], BF16,
                                   is_transpose_dest=True))
            sA2 = [ec(nc.sbuf_tensor(f"sA{i}", [128, WMAX], BF16))
                   for i in range(2)]
            sB2 = [ec(nc.sbuf_tensor(f"sB{i}", [128, WMAX], BF16))
                   for i in range(2)]
            e2 = [ec(nc.sbuf_tensor(f"e{i}", [128, WMAX], BF16))
                  for i in range(2)]
            pT = ec(nc.sbuf_tensor("pT", [128, NTMAX * 256], BF16,
                                   is_transpose_dest=True))
            r8 = ec(nc.sbuf_tensor("r8", [128, 64], F32))
            nm = ec(nc.sbuf_tensor("nm", [128, 1], F32))
            zz = ec(nc.sbuf_tensor("zz", [128, 2], F32))
            rz = ec(nc.sbuf_tensor("rz", [128, 2], F32))
            rrow = ec(nc.sbuf_tensor("rrow", [1, 256], F32))
            ones1 = ec(nc.sbuf_tensor("ones1", [1, 128], F32))
            Rt = ec(nc.sbuf_tensor("Rt", [128, 256], F32))
            iidx = ec(nc.sbuf_tensor("iidx", [128, 1536], F32))
            mask = ec(nc.sbuf_tensor("maskt", [128, 1536], BF16))
            identf = ec(nc.sbuf_tensor("identf", [128, 128], F32))
            identb = ec(nc.sbuf_tensor("identb", [128, 128], BF16))
            pstg = ec(nc.sbuf_tensor("pstg", [128, 128], F32))
            pstgb = ec(nc.sbuf_tensor("pstgb", [128, 128], BF16))
            att = ec(nc.sbuf_tensor("att", [128, 256], BF16))
            NPS = 4
            ps_s = [ec(nc.psum_tensor(f"ps_s{i}", [128, 512], F32))
                    for i in range(NPS)]
            ps_o = ec(nc.psum_tensor("ps_o", [128, 256], F32))
            ps_r = ec(nc.psum_tensor("ps_r", [1, 128], F32))
            ps_R = ec(nc.psum_tensor("ps_R", [128, 256], F32))
            ps_t = ec(nc.psum_tensor("ps_t", [128, 128], BF16))

            T = S // NCORES

            @block.vector
            def _(v):
                sy.wait(v, "v", [])

            @block.scalar
            def _(sc):
                sy.wait(sc, "sc", [])

            @block.tensor
            def _(te):
                sy.wait(te, "t", [])

            @block.gpsimd
            def _(g):
                sy.wait(g, "g", [])

            @block.sync
            def _(s):
                sy.wait(s, "s", [])

            @block.gpsimd
            def _(g):
                T_["iota"] = sy.done(
                    g.iota(iidx[:], pattern=[[-1, 1536]], base=4,
                           channel_multiplier=1,
                           allow_small_or_imprecise_dtypes=True), "g")

            @block.vector
            def _(v):
                sy.wait(v, "v", [T_["iota"]])
                T_["mask"] = sy.done(v.tensor_scalar(out=mask[:], in0=iidx[:],
                                                     scalar1=0.0,
                                                     scalar2=NEG_VIS,
                                                     op0=ALU.is_lt,
                                                     op1=ALU.mult), "v")
                T_["identf"] = sy.done(v.tensor_scalar(out=identf[:],
                                                       in0=iidx[:, 0:128],
                                                       scalar1=4.0, scalar2=None,
                                                       op0=ALU.is_equal), "v")
                T_["ones1"] = sy.done(v.memset(ones1[:], 1.0), "v")
                sy.wait(v, "v", [T_["identf"]])
                T_["identb"] = sy.done(v.tensor_copy(out=identb[:],
                                                     in_=identf[:]), "v")

            st2 = {"sA_free": [[], []], "sB_free": [[], []],
                   "e_free": [[], []],
                   "pT_free": [],
                   "psum_free": [None] * NPS, "zz_free": [], "rz_free": [],
                   "rrow_free": [], "psR_free": [], "pso_free": [],
                   "att_free": [], "psr_free": [], "nm_free": [],
                   "r8_free": [], "vstage_free": []}

            att_stores = []
            for hh in range(2):
                head_gate = [] if hh else [T_["cc1"]]

                @block.gpsimd
                def _(g, hh=hh, head_gate=head_gate):
                    sy.wait(g, "g", head_gate + st2.get("qkT_readers", []))
                    for j in range(NCORES):
                        b = j * 768 + hh * 128
                        tk = sy.done(g.dma_start(out=kT[:, j * T:(j + 1) * T],
                                                 in_=qkvo[b + 256:b + 384, :]),
                                     "ldhk", 16)
                        tq = sy.done(g.dma_start(out=qT[:, j * T:(j + 1) * T],
                                                 in_=qkvo[b:b + 128, :]),
                                     "ldhq", 16)
                    sy.wait(g, "g", st2["vstage_free"] + st2["sA_free"][0])
                    for j in range(NCORES):
                        b = j * 768 + hh * 128
                        tok = sy.done(g.dma_start(
                            out=sA2[0][:, j * T:(j + 1) * T],
                            in_=qkvo[b + 512:b + 640, :]), "ldhv", 16)
                    T_["hload_kq"] = [tk, tq]
                    T_["hload"] = tok

                @block.sync
                def _(s, hh=hh):
                    sy.wait(s, "s", [T_["hload"]] + st2["pT_free"])
                    tr = s.dma_start_transpose(
                        out=vt[:].rearrange("c (ct d) -> c ct d",
                                            d=128)[:, 0:S // 128, :],
                        in_=sA2[0][:, 0:S].rearrange("d (ct c) -> d ct c",
                                                     c=128))
                    T_["vtt"] = sy.done(tr, "dtv", 16)
                    st2["vstage_free"] = [T_["vtt"]]
                    st2["sA_free"][0] = st2["sA_free"][0] + [T_["vtt"]]

                @block.vector
                def _(v, hh=hh, head_gate=head_gate):
                    sy.wait(v, "v", head_gate + [T_.get("pk_stg_free")])
                    T_["pkz"] = sy.done(v.memset(pstg[:], 0.0), "v")

                @block.sync
                def _(s, hh=hh):
                    sy.wait(s, "s", [T_["pkz"]])
                    T_["pkld"] = sy.done(
                        s.dma_start(out=pstg[0:4, :],
                                    in_=ins["pk"][:, hh * 128:(hh + 1) * 128]),
                        "ldb", 16)

                @block.vector
                def _(v, hh=hh):
                    sy.wait(v, "v", [T_["pkld"], T_.get("pkt_prev")])
                    T_["pkb"] = sy.done(v.tensor_copy(out=pstgb[:],
                                                      in_=pstg[:]), "v")

                @block.tensor
                def _(te, hh=hh):
                    sy.wait(te, "t", [T_["pkb"], T_["identb"],
                                      T_.get("pkt_evac_prev")])
                    T_["pkt"] = sy.done(te.transpose(out=ps_t[:], in_=pstgb[:],
                                                     identity=identb[:]), "t")

                @block.vector
                def _(v, hh=hh):
                    sy.wait(v, "v", [T_["pkt"]] + T_["hload_kq"][:1])
                    T_["pk_in_kT"] = sy.done(v.tensor_copy(out=kT[:, S:S + 4],
                                                           in_=ps_t[:, 0:4]),
                                             "v")
                    T_["kT_zpad"] = sy.done(v.memset(kT[:, S + 4:S + 128], 0.0),
                                            "v")
                    sy.wait(v, "v", [T_["vtt"]])
                    T_["vt_z"] = sy.done(v.memset(vt[:, S:S + 128], 0.0), "v")
                    sy.wait(v, "v", [T_["pkt"]])
                    T_["pvz"] = sy.done(v.memset(pstg[:], 0.0), "v")
                T_["pkt_evac_prev"] = T_["pk_in_kT"]

                @block.sync
                def _(s, hh=hh):
                    sy.wait(s, "s", [T_["pvz"]])
                    T_["pvld"] = sy.done(
                        s.dma_start(out=pstg[0:4, :],
                                    in_=ins["pv"][:, hh * 128:(hh + 1) * 128]),
                        "ldb", 16)

                @block.vector
                def _(v, hh=hh):
                    sy.wait(v, "v", [T_["pvld"], T_["vt_z"]])
                    T_["pv_in_vt"] = sy.done(
                        v.tensor_copy(out=vt[0:4, S:S + 128],
                                      in_=pstg[0:4, :]), "v")
                T_["pk_stg_free"] = T_["pv_in_vt"]
                T_["pkt_prev"] = T_["pkb"]

                head_ready = T_["hload_kq"] + [T_["pk_in_kT"],
                                               T_["kT_zpad"]]
                pv_ready = [T_["vtt"], T_["vt_z"], T_["pv_in_vt"]]

                pend_epilogue = [None]

                def flush_epilogue():
                    if pend_epilogue[0] is not None:
                        fn = pend_epilogue[0]
                        pend_epilogue[0] = None
                        fn()

                tiles = []
                for sb in range(nsb):
                    qts = [2 * sb, 2 * sb + 1]
                    Wtrue = 128 * (qts[1] + 2)
                    W = min((Wtrue + 511) // 512 * 512, S + 128)
                    for j, qt in enumerate(qts):
                        tiles.append((sb, j, qt, W))

                def emit_scores_evac(idx):
                    sb, j, qt, W = tiles[idx]
                    pa = (sb * 2 + j) % 2
                    sA = sA2[pa]
                    nblk = (W + 511) // 512
                    mm_toks = []
                    ev_toks = []

                    @block.tensor
                    def _(te, qt=qt, nblk=nblk, W=W, mm_toks=mm_toks):
                        sy.wait(te, "t", head_ready)
                        sc_base = sy.counts["sc"]
                        for b in range(nblk):
                            c0, c1 = b * 512, min((b + 1) * 512, W)
                            if b < NPS:
                                sy.wait(te, "t", [st2["psum_free"][b % NPS]])
                            else:
                                sy.wait(te, "t", [("sc", sc_base + b - NPS + 1)])
                            mm = te.matmul(
                                out=ps_s[b % NPS][:, 0:c1 - c0],
                                lhsT=qT[:, qt * 128:(qt + 1) * 128],
                                rhs=kT[:, c0:c1],
                                start=True, stop=True)
                            mm_toks.append(sy.done(mm, "t"))
                            st2["qkT_readers"] = [mm_toks[-1]]

                    @block.scalar
                    def _(sc, nblk=nblk, W=W, sA=sA, pa=pa,
                          mm_toks=mm_toks, ev_toks=ev_toks):
                        sy.wait(sc, "sc", st2["sA_free"][pa])
                        for b in range(nblk):
                            c0, c1 = b * 512, min((b + 1) * 512, W)
                            sy.wait(sc, "sc", [mm_toks[b]])
                            ev = sc.activation(out=sA[:, c0:c1],
                                               in_=ps_s[b % NPS][:, 0:c1 - c0],
                                               func=AF.Copy, bias=0.0,
                                               scale=1.0)
                            tok = sy.done(ev, "sc")
                            st2["psum_free"][b % NPS] = tok
                            ev_toks.append(tok)
                    return mm_toks, ev_toks

                pend_se = emit_scores_evac(0)
                gate_toks = []
                tr_toks = []
                for idx in range(len(tiles)):
                    sb, j, qt, W = tiles[idx]
                    pa = (sb * 2 + j) % 2
                    sA = sA2[pa]
                    sB = sB2[pa]
                    e = e2[pa]
                    Wx = min(128 * (qt + 2), S + 128)
                    ntc = W // 128
                    if j == 0:
                        gate_toks = []
                        tr_toks = []
                    mm_toks, ev_toks = pend_se

                    @block.vector
                    def _(v, j=j, qt=qt, W=W, Wx=Wx, ev_toks=ev_toks,
                          sA=sA, pa=pa, e=e):
                        mw = Wx - 128 * qt
                        sy.wait(v, "v", ev_toks + [T_["mask"]])
                        tmask = sy.done(v.tensor_add(sA[:, Wx - mw:Wx],
                                                     sA[:, Wx - mw:Wx],
                                                     mask[:, 0:mw]), "v")
                        st2["sA_free"][pa] = []
                        if Wx < W:
                            sy.wait(v, "v", st2["e_free"][pa])
                            tz = sy.done(v.memset(e[:, Wx:W], 0.0), "v")
                            st2["e_free"][pa] = [tz]
                        sy.wait(v, "v", [tmask] + st2["r8_free"])
                        tmx = sy.done(v.max(out=r8[:, 0:8],
                                            in_=sA[:, 0:Wx]), "v")
                        sy.wait(v, "v", [tmx] + st2["nm_free"])
                        tnm = sy.done(v.tensor_scalar_mul(nm[:], r8[:, 0:1],
                                                          -1.0), "v")
                        sy.wait(v, "v", st2["sB_free"][pa])
                        tmr = sy.done(v.match_replace(
                            out=sB[:, 0:Wx], in_to_replace=r8[:, 0:8],
                            in_values=sA[:, 0:Wx], imm_value=SENT), "v")
                        for rnd in range(1, 8):
                            sy.wait(v, "v", [tmr])
                            tmx = sy.done(v.max(out=r8[:, 8 * rnd:8 * rnd + 8],
                                                in_=sB[:, 0:Wx]), "v")
                            sy.wait(v, "v", [tmx])
                            tmr = sy.done(v.match_replace(
                                out=sB[:, 0:Wx],
                                in_to_replace=r8[:, 8 * rnd:8 * rnd + 8],
                                in_values=sB[:, 0:Wx], imm_value=SENT), "v")
                        st2["sB_free"][pa] = []
                        st2["r8_free"] = [tmr]
                        T_["tnm"], T_["tmr8"] = tnm, tmr

                    if idx + 1 < len(tiles):
                        pend_se = emit_scores_evac(idx + 1)

                    @block.scalar
                    def _(sc, W=W, Wx=Wx, sA=sA, e=e, pa=pa):
                        sy.wait(sc, "sc", [T_["tnm"]] + st2["e_free"][pa])
                        ex = sc.activation(out=e[:, 0:Wx], in_=sA[:, 0:Wx],
                                           func=AF.Exp, bias=nm[:],
                                           scale=1.0)
                        T_["exp"] = sy.done(ex, "sc")
                        st2["e_free"][pa] = []
                        st2["nm_free"] = [T_["exp"]]
                        st2["sA_free"][pa] = [T_["exp"], T_["tmr8"]]

                    @block.vector
                    def _(v, j=j, W=W, Wx=Wx, e=e, sB=sB, pa=pa):
                        sy.wait(v, "v", [T_["exp"], T_["tmr8"]]
                                + st2["zz_free"])
                        g_ = v.scalar_tensor_tensor(
                            out=e[:, 0:Wx], in0=sB[:, 0:Wx],
                            scalar=SENT_CMP,
                            in1=e[:, 0:Wx], op0=ALU.is_lt, op1=ALU.mult,
                            accum_out=zz[:, j:j + 1])
                        T_["gate"] = sy.done(g_, "v")
                        gate_toks.append(T_["gate"])
                        st2["sB_free"][pa] = [T_["gate"]]

                    if j == 0:
                        flush_epilogue()

                    @block.sync
                    def _(s, j=j, W=W, ntc=ntc, e=e, pa=pa):
                        sy.wait(s, "s", [T_["gate"]] + st2["pT_free"])
                        tr = s.dma_start_transpose(
                            out=pT[:].rearrange("c (t j q) -> c t j q",
                                                j=2, q=128)[:, 0:ntc, j, :],
                            in_=e[:, 0:W].rearrange("q (t c) -> q t c",
                                                    c=128))
                        T_["tr"] = sy.done(tr, f"dt{pa}", 16)
                        tr_toks.append(T_["tr"])
                        st2["e_free"][pa] = [T_["tr"]]

                    if j != 1:
                        continue

                    @block.vector
                    def _(v, gate_toks=gate_toks):
                        sy.wait(v, "v", gate_toks + st2["rz_free"])
                        T_["rz"] = sy.done(v.reciprocal(out=rz[:], in_=zz[:]),
                                           "v")
                        st2["zz_free"] = [T_["rz"]]

                    def make_epilogue(hh=hh, sb=sb, ntc=ntc,
                                      tr_toks=tr_toks, rz_tok=None):
                        rz_tok = T_["rz"]

                        def emit():
                            for j in range(2):
                                @block.tensor
                                def _(te, j=j):
                                    sy.wait(te, "t", [rz_tok, T_["identf"]]
                                            + st2["psr_free"])
                                    T_[f"psr{j}"] = sy.done(
                                        te.transpose(out=ps_r[:],
                                                     in_=rz[:, j:j + 1],
                                                     identity=identf[:]), "t")

                                @block.scalar
                                def _(sc, j=j):
                                    sy.wait(sc, "sc", [T_[f"psr{j}"]]
                                            + st2["rrow_free"])
                                    ev = sc.activation(
                                        out=rrow[0:1, j * 128:(j + 1) * 128],
                                        in_=ps_r[:], func=AF.Copy, bias=0.0,
                                        scale=1.0)
                                    T_[f"rrow{j}"] = sy.done(ev, "sc")
                                    st2["psr_free"] = [T_[f"rrow{j}"]]

                            st2["rz_free"] = [T_["psr0"], T_["psr1"]]
                            st2["rrow_free"] = []

                            @block.tensor
                            def _(te):
                                sy.wait(te, "t", [T_["rrow0"], T_["rrow1"],
                                                  T_["ones1"]]
                                        + st2["psR_free"])
                                for j in range(2):
                                    mm = te.matmul(
                                        out=ps_R[:, j * 128:(j + 1) * 128],
                                        lhsT=ones1[:],
                                        rhs=rrow[0:1, j * 128:(j + 1) * 128],
                                        start=True, stop=True)
                                T_["Rmm"] = sy.done(mm, "t")
                                st2["rrow_free"] = [T_["Rmm"]]
                                sy.wait(te, "t", tr_toks + st2["pso_free"]
                                        + pv_ready)
                                for ct in range(ntc):
                                    mm = te.matmul(
                                        out=ps_o[:],
                                        lhsT=vt[:, ct * 128:(ct + 1) * 128],
                                        rhs=pT[:, ct * 256:(ct + 1) * 256],
                                        start=(ct == 0),
                                        stop=(ct == ntc - 1))
                                T_["pv"] = sy.done(mm, "t")
                                st2["pT_free"] = [T_["pv"]]
                            Rmm = T_["Rmm"]
                            pv = T_["pv"]

                            @block.vector
                            def _(v):
                                sy.wait(v, "v", [Rmm] + st2.get("Rt_free", []))
                                tR = sy.done(v.tensor_copy(out=Rt[:],
                                                           in_=ps_R[:]), "v")
                                st2["psR_free"] = [tR]
                                sy.wait(v, "v", [tR, pv] + st2["att_free"])
                                ta = sy.done(v.tensor_mul(att[:], ps_o[:],
                                                          Rt[:]), "v")
                                T_["att"] = ta
                                st2["pso_free"] = [ta]
                                st2["Rt_free"] = [ta]

                            @block.sync
                            def _(s):
                                sy.wait(s, "s", [T_["att"]])
                                spr = T // 256
                                r, c0 = sb // spr, (sb % spr) * 256
                                stt = s.dma_start(
                                    out=ati[r * 256 + hh * 128:
                                            r * 256 + (hh + 1) * 128,
                                            c0:c0 + 256],
                                    in_=att[:])
                                st2_t = sy.done(stt, "sta", 16)
                                T_["store"] = st2_t
                                att_stores.append(st2_t)
                                st2["att_free"] = [st2_t]
                        return emit

                    pend_epilogue[0] = make_epilogue()

                flush_epilogue()
                T_["headdone"] = T_["store"]

            @block.gpsimd
            def _(g):
                wo_toks = []
                for i in range(HC):
                    wo_toks.append(sy.done(
                        g.dma_start(out=wobf[i * 128:(i + 1) * 128, :],
                                    in_=ins["Wo"][i * 128:(i + 1) * 128, :]),
                        f"ldg{i % 2}", 16))
                T_["wocast"] = wo_toks[-2:]

            @block.gpsimd
            def _(g):
                sy.wait(g, "g", att_stores)
                cc = g.collective_compute("AllToAll", ALU.bypass,
                                          replica_groups=RG,
                                          ins=[ati.opt()], outs=[ato.opt()])
                T_["cc2"] = sy.done(cc, "cc", 1)

        sy.set_floor()
        with ExitStack() as ctx:
            ec = ctx.enter_context
            aT = ec(nc.sbuf_tensor("aTs", [128, HC * T], BF16))
            Wbf3 = ec(nc.sbuf_tensor("Wbf3", [128, HC * E], BF16))

            bstg3 = ec(nc.sbuf_tensor("bstg3", [1, E], F32))
            bob = ec(nc.sbuf_tensor("bob", [1, E], BF16))
            ones3 = ec(nc.sbuf_tensor("ones3", [1, 128], BF16))
            pstg3 = [ec(nc.sbuf_tensor(f"pstg3{i}", [128, E], F32))
                     for i in range(2)]
            outf3 = [ec(nc.sbuf_tensor(f"outf3{i}", [128, E], F32))
                     for i in range(2)]
            NPS = 4
            psl3 = [ec(nc.psum_tensor(f"psl3{i}", [128, 512], F32))
                    for i in range(NPS)]

            st3 = {"psum_free": [None] * NPS,
                   "pstg_free": [[], []], "outf_free": [[], []]}

            @block.vector
            def _(v):
                sy.wait(v, "v", [])

            @block.scalar
            def _(sc):
                sy.wait(sc, "sc", [])

            @block.tensor
            def _(te):
                sy.wait(te, "t", [])

            @block.gpsimd
            def _(g):
                sy.wait(g, "g", [])

            @block.sync
            def _(s):
                sy.wait(s, "s", [])

            @block.vector
            def _(v):
                T_["ones3"] = sy.done(v.memset(ones3[:], 1.0), "v")

            @block.gpsimd
            def _(g):
                sy.wait(g, "g", [T_["cc2"]])
                for i in range(4):
                    tok = sy.done(g.dma_start(
                        out=aT[:].rearrange("c (hc t) -> c hc t",
                                            t=T)[:, i * 4:(i + 1) * 4, :],
                        in_=ato.rearrange("(hc c) t -> c hc t",
                                          c=128)[:, i * 4:(i + 1) * 4, :]),
                        "lda", 16)
                T_["ald"] = tok

            @block.sync
            def _(s):
                sy.wait(s, "s", [T_["pstore"]])
                T_["bld3"] = sy.done(s.dma_start(out=bstg3[:], in_=ins["bo"]),
                                     "ldb", 16)

            @block.vector
            def _(v):
                sy.wait(v, "v", [T_["bld3"]])
                T_["bo"] = sy.done(v.tensor_copy(out=bob[:], in_=bstg3[:]), "v")

            wls = []
            for i in range(HC):
                if i % 2 == 0:
                    @block.sync
                    def _(s, i=i):
                        if i == 0:
                            sy.wait(s, "s", T_["wocast"])
                        wls.append(sy.done(
                            s.dma_start(out=Wbf3[:, i * E:(i + 1) * E],
                                        in_=wobf[i * 128:(i + 1) * 128, :]),
                            "ldw0", 16))
                else:
                    @block.scalar
                    def _(sc, i=i):
                        if i == 1:
                            sy.wait(sc, "sc", T_["wocast"])
                        wls.append(sy.done(
                            sc.dma_start(out=Wbf3[:, i * E:(i + 1) * E],
                                         in_=wobf[i * 128:(i + 1) * 128, :]),
                            "ldw1", 16))

            for tt in range(T // 128):
                p = tt % 2

                @block.scalar
                def _(sc, tt=tt, p=p):
                    sy.wait(sc, "sc", st3["pstg_free"][p])
                    T_["pld"] = sy.done(
                        sc.dma_start(out=pstg3[p][:],
                                     in_=part[tt * 128:(tt + 1) * 128, :]),
                        f"lp{p}", 16)

                cmb = []
                for jb in range(E // 512):
                    bank = (tt * 4 + jb) % NPS

                    @block.tensor
                    def _(te, tt=tt, jb=jb, bank=bank):
                        sy.wait(te, "t", wls + [T_["ald"], T_["bo"], T_["ones3"],
                                                st3["psum_free"][bank]])
                        for hc in range(HC):
                            te.matmul(
                                out=psl3[bank][:],
                                lhsT=aT[:, hc * T + tt * 128:
                                        hc * T + (tt + 1) * 128],
                                rhs=Wbf3[:, hc * E + jb * 512:
                                         hc * E + (jb + 1) * 512],
                                start=(hc == 0), stop=False)
                        mm = te.matmul(out=psl3[bank][:], lhsT=ones3[:],
                                       rhs=bob[:, jb * 512:(jb + 1) * 512],
                                       start=False, stop=True)
                        T_["omm"] = sy.done(mm, "t")

                    @block.vector
                    def _(v, tt=tt, jb=jb, bank=bank, p=p):
                        sy.wait(v, "v", [T_["omm"], T_["pld"]]
                                + st3["outf_free"][p])
                        o1 = sy.done(v.tensor_add(
                            outf3[p][:, jb * 512:(jb + 1) * 512], psl3[bank][:],
                            pstg3[p][:, jb * 512:(jb + 1) * 512]), "v")
                        st3["psum_free"][bank] = o1
                        cmb.append(o1)

                @block.sync
                def _(s, tt=tt, p=p, cmb=cmb):
                    sy.wait(s, "s", cmb)
                    stt = s.dma_start(out=out_ap[tt * 128:(tt + 1) * 128, :],
                                      in_=outf3[p][:])
                    tok = sy.done(stt, f"st{p}", 16)
                    st3["outf_free"][p] = [tok]
                    st3["pstg_free"][p] = [tok] + cmb
                    T_[f"ostore{p}"] = tok

            @block.sync
            def _(s):
                sy.wait(s, "s", [T_["ostore0"], T_["ostore1"]])

    sy.close()
    return nc


S_FULL = 8192
E_FULL = 2048
NCORES = 8


def kernel(L0_attn_x, L0_attn_Wq, L0_attn_bq, L0_attn_Wk, L0_attn_bk,
           L0_attn_Wv, L0_attn_bv, L0_attn_Wo, L0_attn_bo, L0_attn_pk,
           L0_attn_pv, L0_ffn_A, L0_ffn_B, L0_ffn_W, L0_ffn_bias):
    S, E = S_FULL, E_FULL
    Tl = S // NCORES
    x = np.ascontiguousarray(np.asarray(L0_attn_x, dtype=np.float32)
                             .reshape(S, E))
    A2T = np.ascontiguousarray(
        np.asarray(L0_ffn_A, dtype=np.float32).reshape(1024, E).T)
    B2 = np.ascontiguousarray(
        np.asarray(L0_ffn_B, dtype=np.float32).transpose(0, 2, 1).reshape(1024, E))
    WT = np.ascontiguousarray(np.asarray(L0_ffn_W, dtype=np.float32).T)
    Wq = np.ascontiguousarray(np.asarray(L0_attn_Wq, dtype=np.float32))
    Wk = np.ascontiguousarray(np.asarray(L0_attn_Wk, dtype=np.float32))
    Wv = np.ascontiguousarray(np.asarray(L0_attn_Wv, dtype=np.float32))
    Wo = np.ascontiguousarray(np.asarray(L0_attn_Wo, dtype=np.float32))
    bs = {"bq": L0_attn_bq, "bk": L0_attn_bk, "bv": L0_attn_bv,
          "fb": L0_ffn_bias, "bo": L0_attn_bo}
    bs = {k: np.ascontiguousarray(np.asarray(v, dtype=np.float32)
                                  .reshape(1, E)) for k, v in bs.items()}
    pk = np.ascontiguousarray(np.asarray(L0_attn_pk, dtype=np.float32)
                              .reshape(4, E))
    pv = np.ascontiguousarray(np.asarray(L0_attn_pv, dtype=np.float32)
                              .reshape(4, E))

    nc = build_fused(S, NCORES)
    maps = []
    for c in range(NCORES):
        m = {"x": np.ascontiguousarray(x[c * Tl:(c + 1) * Tl]),
             "Wq": Wq, "Wk": Wk, "Wv": Wv, "WT": WT, "Wo": Wo,
             "A2T": A2T, "B2": B2,
             "pk": np.ascontiguousarray(pk[:, c * 256:(c + 1) * 256]),
             "pv": np.ascontiguousarray(pv[:, c * 256:(c + 1) * 256])}
        m.update(bs)
        maps.append(m)
    res = run_bass_kernel_spmd(nc, maps, core_ids=list(range(NCORES)))
    out = np.concatenate([np.asarray(res.results[c]["out"])
                          for c in range(NCORES)], axis=0)
    return out.reshape(1, S, E).astype(np.float32)
